# version 19
# baseline (speedup 1.0000x reference)
import os
import numpy as np
import ml_dtypes
import concourse.bass as bass
import concourse.mybir as mybir
from concourse.bass_utils import run_bass_kernel_spmd
from contextlib import ExitStack

F32 = mybir.dt.float32
BF16 = mybir.dt.bfloat16
I8 = mybir.dt.int8
AF = mybir.ActivationFunctionType
ALU = mybir.AluOpType
AXX = mybir.AxisListType.X
BF = ml_dtypes.bfloat16

T = 8192
D = 1024
C = 512
NT = 4
H = 8
DH = 64
NM = 22
HIDP = 2816
NEG = -1.0e30


def host_rotary_tables():
    half = np.arange(0, DH, 2, dtype=np.float64)
    inv_freq = 1.0 / 10000.0 ** (half / DH)
    t = np.arange(T, dtype=np.float64)
    freqs = np.concatenate([t[:, None] * inv_freq[None, :]] * 2, axis=1)
    base = (half + 0.4 * DH) / (1.4 * DH)
    power = (t - T // 2) / 256.0
    scale = np.concatenate([base[None, :] ** power[:, None]] * 2, axis=1)
    cos, sin = np.cos(freqs), np.sin(freqs)
    qA, qB = cos * scale, sin * scale
    kA, kB = cos / scale, sin / scale
    qBs = qB.copy(); qBs[:, 0:32] *= -1.0
    kBs = kB.copy(); kBs[:, 0:32] *= -1.0
    return [x.astype(BF) for x in (qA, qBs, kA, kBs)]


def host_masks():
    p = np.arange(128)[:, None]
    c = np.arange(640)[None, :]
    band = (c >= p) & (c <= p + 512)
    return np.where(band, 0.0, NEG).astype(BF)


def host_mcut():
    p = np.arange(128)[:, None]
    c = np.arange(512)[None, :]
    band = (c >= p) & (c <= p + 512)
    cuts = [np.where(band & (c >= 512 - 128 * r), 0.0, NEG) for r in range(4)]
    return np.stack(cuts).astype(BF)


def build(nch):
    KLVL = int(os.environ.get("KLVL", "6"))
    KP0 = int(os.environ.get("KP0", "2"))
    KD = int(os.environ.get("KD", "4"))
    nc = bass.Bass("TRN2", target_bir_lowering=False)
    es = ExitStack()

    def din(name, shape, dt=F32):
        return nc.dram_tensor(name, shape, dt, kind="ExternalInput")

    u_h = din("u", [T, D])
    bwT_h = din("bwT", [D, 256])
    cwT_h = din("cwT", [256, D])
    llam_h = din("llam", [128, 2])
    d_h = din("dvec", [128, 8])
    wqkv_h = din("wqkv", [2, D, 1536])
    wo_h = din("wo", [2, 512, D])
    w1a_h = din("w1a", [2, D, HIDP])
    w1g_h = din("w1g", [2, D, HIDP])
    w2p_h = din("w2p", [2, HIDP, D])
    b1a_h = din("b1a", [2, 128, NM])
    b1g_h = din("b1g", [2, 128, NM])
    b2_h = din("b2", [2, 128, 8])
    lglb_h = din("lglb", [2, 2, D])
    qs_h = din("qsc", [2, 128, 512])
    ks_h = din("ksc", [2, 128, 512])
    idb_h = din("idb", [128, 128], BF16)
    idf_h = din("idf", [128, 128])
    onescol_h = din("onescol", [128, 1], BF16)
    onesrow_h = din("onesrow", [1, 128], BF16)
    ones2r_h = din("ones2r", [1, 512], BF16)
    masks_h = din("masks", [128, 640], BF16)
    mcut_h = din("mcut", [4, 128, 512], BF16)
    rtab_h = din("rtab", [4, T, DH], BF16)
    out_h = nc.dram_tensor("out", [T, D], F32, kind="ExternalOutput")

    scr_wqkv = nc.dram_tensor("scr_wqkv", [2, 8, 128, 1536], BF16)
    scr_wo = nc.dram_tensor("scr_wo", [2, 4, 128, D], BF16)
    scr_w1 = nc.dram_tensor("scr_w1", [2, NM, 128, 2, 8, 128], BF16)
    scr_w2 = nc.dram_tensor("scr_w2", [2, 2, 11, 128, 2, 512], BF16)

    A = nc.alloc_sbuf_tensor
    X1 = A("X1", [128, 8, C], F32)
    Y = A("Yb", [128, 8, C + 1], F32)
    STG = A("STG", [128, 8, C + 1], BF16)
    HBF = A("HBF", [128, 2, C], BF16)
    KF = [A(f"KF{l}", [128, 4, 1024], BF16) for l in range(2)]
    VT = [A(f"VT{l}", [128, 8, 512], BF16) for l in range(2)]
    OB = A("OB", [128, 4, C], BF16)
    TMPF = A("TMPF", [128, C], F32)
    USTG = A("USTG", [128, 4, D], F32)
    SCAN = USTG.ap()[:, 0].rearrange("p (a b) -> p a b", b=C)
    TMPG = USTG.ap()[:, 1, 0:C]
    OUTB = A("OUTB", [128, 2, D], F32)
    WQKVR = A("WQKVR", [128, 2, 2, 1536], BF16)
    WO = A("WOb", [128, 4, D], BF16)
    W1R = A("W1R", [128, 2, 2, 8, 128], BF16)
    W2R = A("W2R", [128, 2, 2, 512], BF16)
    BWT = A("BWT", [128, 8, 256], BF16)
    CWT = A("CWT", [128, 2, D], BF16)
    RT = A("RT", [128, 2, 4, 4, DH], BF16)
    QS8 = A("QS8", [128, 2, 512], BF16)
    KS8 = A("KS8", [128, 2, 512], BF16)
    MASKS = A("MASKS", [128, 1, 640], BF16)
    MCUT = A("MCUT", [128, 4, 512], BF16)
    IDB = A("IDB", [128, 128], BF16)
    IDF = A("IDF", [128, 128], F32)
    ONESC = A("ONESC", [128, 1], BF16)
    ONESR = A("ONESR", [1, 128], BF16)
    LGLB = A("LGLB", [2, 2, D], BF16)
    B1A = A("B1A", [128, 2, NM], F32)
    B1G = A("B1G", [128, 2, NM], F32)
    B2 = A("B2", [128, 2, 8], F32)
    DV = A("DV", [128, 8], F32)
    D1 = A("D1", [128, 8], F32)
    LLAM = A("LLAM", [128, 2], F32)
    LAM = A("LAM", [128, 2], F32)
    SSMC = A("SSMC", [128, 2], F32)
    C1 = [A(f"C1_{l}", [128, 4, 1], F32) for l in range(2)]
    C2 = [A(f"C2_{l}", [128, 4, 1], F32) for l in range(2)]
    NRM = A("NRM", [128, 2, 16], F32)
    RS = A("RS", [128, 2, 16], F32)
    MAXB = A("MAXB", [128, 4], F32)
    NM8 = A("NM8", [128, 4], F32)
    DEN = A("DEN", [128, 4], F32)
    RCP = A("RCP", [128, 4], F32)
    MU = A("MU", [1, 512], F32)
    E2 = A("E2", [1, 512], F32)
    VARR = A("VARR", [1, 512], F32)
    RCPV = A("RCPV", [1, 512], F32)
    RSTDF = A("RSTDF", [1, 512], F32)
    RSTD16 = A("RSTD16", [1, 512], BF16)
    SROW2 = A("SROW2", [2, 512], BF16)
    SUMC = A("SUMC", [128, 4], F32)
    SQC = A("SQC", [128, 4], F32)
    MUC = A("MUC", [128, 4], F32)
    E2C = A("E2C", [128, 4], F32)
    VARC = A("VARC", [128, 4], F32)
    RCPC = A("RCPC", [128, 4], F32)
    RSTDC = A("RSTDC", [128, 4], F32)
    NMRC = A("NMRC", [128, 4], F32)

    AR = A("AR", [128, 32768], I8)
    arv = AR.ap().bitcast(BF16)

    def arview(off, *shape):
        n = 1
        for s in shape:
            n *= s
        v = arv[:, off:off + n]
        if len(shape) == 2:
            v = v.rearrange("p (a b) -> p a b", b=shape[1])
        return v

    HBv = arview(0, NM, 512)
    XNv = arview(11264, 8, 512)
    GEGv = arview(15360, 2, 512)
    QFv = arview(0, 4, 512)
    QTOKv = arview(2048, 4, 512)
    KTOKv = arview(4096, 4, 512)
    AUv = arview(6144, 4, 640)
    ATv = arview(8704, 2, 640)
    TQ1v = arview(9984, 2, 512)
    TR2v = arview(11008, 2, 512)
    QSQv = arview(12032, 4, 512)
    SQ16v = arview(0, 8, 512)
    SQDUMP = arview(0, 1024)

    PSL = [nc.alloc_psum_tensor(f"PSL{i}", [128, 1024], F32) for i in range(2)]
    PB = [nc.alloc_psum_tensor(f"PB{i}", [128, 512], F32) for i in range(4)]

    def pslh(i, half):
        return PSL[i][:, half * 512:(half + 1) * 512]

    def pbb16(j):
        return PB[j].ap().bitcast(BF16)

    def pslb16(i):
        return PSL[i].ap().bitcast(BF16)

    sems = {}
    for name in ["dIN", "dSET", "dGC", "dPRO", "dOUT0", "dOUT1",
                 "dQK", "dWO", "dW1", "dW2",
                 "tM", "vV", "sS", "gG",
                 "aL", "aM", "aE", "aR", "aT2", "aV2", "aA", "aO",
                 "mT", "mS", "mV", "oV", "oS", "oG", "aRc", "aV2s"]:
        sems[name] = [es.enter_context(nc.semaphore(name)), 0]

    ops = {k: [] for k in ["sync", "gpsimd", "tensor", "vector", "scalar"]}
    SELF = {"vector": "oV", "scalar": "oS", "gpsimd": "oG"}

    last_evt = {"vector": None, "scalar": None, "gpsimd": None}

    def E(eng, f, inc=None, n=1, dma=False, chain=True):
        self_sem = SELF.get(eng)
        chain = self_sem is not None and not dma and chain
        if chain and last_evt[eng] is not None:
            ph_, pv_ = last_evt[eng]
            ops[eng].append(lambda e, ph_=ph_, pv_=pv_: e.wait_ge(ph_, pv_))
        if inc is not None:
            sems[inc][1] += n
            h, v = sems[inc][0], sems[inc][1]
            if chain:
                last_evt[eng] = (h, v)
            ops[eng].append(lambda e, f=f, h=h, n=n: f(e).then_inc(h, n))
        elif chain:
            sems[self_sem][1] += 1
            h, v = sems[self_sem][0], sems[self_sem][1]
            last_evt[eng] = (h, v)
            ops[eng].append(lambda e, f=f, h=h: f(e).then_inc(h, 1))
        else:
            ops[eng].append(f)
        return sems[inc][1] if inc is not None else None

    def W(eng, name, val=None):
        v = sems[name][1] if val is None else val
        h = sems[name][0]
        ops[eng].append(lambda e, h=h, v=v: e.wait_ge(h, v))

    def cnt(name):
        return sems[name][1]

    def WLAST(eng, src_eng):
        evt = last_evt[src_eng]
        if evt is not None:
            h, v = evt
            ops[eng].append(lambda e, h=h, v=v: e.wait_ge(h, v))

    for dst, src in [(IDB[:], idb_h.ap()), (IDF[:], idf_h.ap()),
                     (ONESC[:], onescol_h.ap()), (ONESR[:], onesrow_h.ap()),
                     (SROW2[1:2, :], ones2r_h.ap()),
                     (MASKS.ap()[:, 0], masks_h.ap()),
                     (MCUT.ap(), mcut_h.ap().rearrange("m p c -> p m c")),
                     (DV[:], d_h.ap()), (LLAM[:], llam_h.ap()),
                     (B1A.ap(), b1a_h.ap().rearrange("l p m -> p l m")),
                     (B1G.ap(), b1g_h.ap().rearrange("l p m -> p l m")),
                     (B2.ap(), b2_h.ap().rearrange("l p m -> p l m"))]:
        E("sync", lambda e, d=dst, s=src: e.dma_start(d, s), "dSET", 16)
    SETUP_DIN = cnt("dSET")

    for dst, src in [
        (BWT.ap(), bass.AP(bwT_h, 0, [[256, 128], [128 * 256, 8], [1, 256]])),
        (CWT.ap(), bass.AP(cwT_h, 0,
                           [[1024, 128], [128 * 1024, 2], [1, 1024]])),
        (QS8.ap(), qs_h.ap().rearrange("l p c -> p l c")),
        (KS8.ap(), ks_h.ap().rearrange("l p c -> p l c")),
        (LGLB.ap(), lglb_h.ap()),
    ]:
        E("gpsimd", lambda e, d=dst, s=src: e.dma_start(d, s), "dGC", 16,
          dma=True)
    SETUP_DW = cnt("dGC")

    zb = [SSMC[:], C1[0][:], C1[1][:], C2[0][:], C2[1][:],
          KF[0][:, :, 0:512], KF[1][:, :, 0:512],
          VT[0][:, 0:4, :], VT[1][:, 0:4, :]]
    for i, buf in enumerate(zb):
        E("gpsimd", lambda e, b=buf: e.memset(b, 0.0),
          "gG" if i == len(zb) - 1 else None)
    G_INIT = cnt("gG")

    W("vector", "dSET", SETUP_DIN)
    E("vector", lambda e: e.tensor_scalar_add(D1[:], DV[:], 1.0), "vV")
    W("vector", "dGC", SETUP_DW)
    E("vector", lambda e: e.tensor_scalar_mul(
        QS8.ap().rearrange("p a b -> p (a b)"),
        QS8.ap().rearrange("p a b -> p (a b)"), 8.0), "vV")
    W("scalar", "dSET", SETUP_DIN)
    SS_LAM = E("scalar", lambda e: e.activation(LAM[:], LLAM[:], AF.Sigmoid),
               "sS")

    for l in range(2):
        for k in range(8):
            E("gpsimd", lambda e, l=l, k=k: e.dma_start(
                scr_wqkv.ap()[l, k],
                wqkv_h.ap()[l, k * 128:(k + 1) * 128, :]), "dPRO", 16,
              dma=True)
        for k in range(4):
            E("gpsimd", lambda e, l=l, k=k: e.dma_start(
                scr_wo.ap()[l, k],
                wo_h.ap()[l, k * 128:(k + 1) * 128, :]), "dPRO", 16,
              dma=True)
        for m in range(NM):
            for s, wsrc in ((0, w1a_h), (1, w1g_h)):
                E("gpsimd", lambda e, l=l, m=m, s=s, wsrc=wsrc: e.dma_start(
                    scr_w1.ap()[l, m][:, s],
                    bass.AP(wsrc, l * D * HIDP + m * 128,
                            [[HIDP, 128], [128 * HIDP, 8], [1, 128]])),
                  "dPRO", 16, dma=True)
        for og in range(2):
            for kq in range(NM):
                E("gpsimd", lambda e, l=l, og=og, kq=kq: e.dma_start(
                    scr_w2.ap()[l, og, kq // 2][:, kq % 2],
                    bass.AP(w2p_h, l * HIDP * D + kq * 128 * D + og * 512,
                            [[D, 128], [1, 512]])), "dPRO", 16, dma=True)
    PRO_ALL = cnt("dPRO")

    bank4 = [PB[0][:], PB[1][:], PB[2][:], PB[3][:]]

    def ln_rows(srct):
        W("vector", "tM")
        E("vector", lambda e, srct=srct: e.tensor_scalar_mul(
            MU[:], srct[0:1, 0:512], 1.0 / 1024.0))
        E("vector", lambda e, srct=srct: e.tensor_scalar_mul(
            E2[:], srct[0:1, 512:1024], 1.0 / 1024.0))
        E("vector", lambda e: e.tensor_mul(VARR[:], MU[:], MU[:]))
        E("vector", lambda e: e.tensor_sub(VARR[:], E2[:], VARR[:]))
        E("vector", lambda e: e.tensor_scalar_add(VARR[:], VARR[:], 1e-5))
        E("vector", lambda e: e.reciprocal(RCPV[:], VARR[:]), "vV")
        W("scalar", "vV")
        E("scalar", lambda e: e.activation(RSTDF[:], RCPV[:], AF.Sqrt))
        E("scalar", lambda e: e.activation(RSTD16[:], RCPV[:], AF.Sqrt), "sS")
        W("vector", "sS")
        E("vector", lambda e: e.scalar_tensor_tensor(
            SROW2[0:1, :], MU[:], -1.0, RSTDF[:], ALU.mult, ALU.mult), "vV")

    wo_mm_mark = [None]
    p0_tm_mark = [None]
    out_dma_marks = {0: None, 1: None}
    qkv_marks = []

    def emit_input_load(eng, cit):
        E(eng, lambda e, t0c=cit * C: e.dma_start(
            USTG.ap(), bass.AP(u_h, t0c * D,
                               [[D, 128], [128 * D, 4], [1, D]])),
          "dIN", 16, dma=True)
        for mtb in range(4):
            E(eng, lambda e, rbt=cit % 2, t0c=cit * C, mtb=mtb: e.dma_start(
                RT.ap()[:, rbt, mtb],
                bass.AP(rtab_h, mtb * T * DH + t0c * DH,
                        [[DH, 128], [128 * DH, 4], [1, DH]])),
              "dIN", 16, dma=True)

    for ci in range(nch):
        t0 = ci * C
        rb = ci % 2

        if KLVL >= -1:
            if ci == 0:
                emit_input_load("sync", 0)
            W("tensor", "dIN", 80 * (ci + 1))
            if ci == 0:
                W("tensor", "dSET", SETUP_DIN)
            else:
                WLAST("tensor", "vector")
                WLAST("tensor", "scalar")
            p0_vv, p0_ss = [], []
            for ft in range(8 if KP0 >= 2 else 0):
                bank = bank4[ft % 4]
                if ft >= 4:
                    W("tensor", "vV", p0_vv[ft - 4])
                    W("tensor", "sS", p0_ss[ft - 4])
                for tt in range(NT):
                    E("tensor", lambda e, b=bank, ft=ft, tt=tt: e.transpose(
                        b[:, tt * 128:(tt + 1) * 128],
                        USTG[:, tt, ft * 128:(ft + 1) * 128], IDF[:]),
                      "tM" if tt == NT - 1 else None)
                W("vector", "tM")
                p0_vv.append(E("vector", lambda e, b=bank, ft=ft: e.tensor_copy(
                    X1[:, ft, :], b), "vV"))
                W("scalar", "vV", p0_vv[-1])
                p0_ss.append(E("scalar", lambda e, b=bank, ft=ft: e.copy(
                    STG[:, ft, 1:C + 1], b), "sS"))
            p0_tm_mark[0] = cnt("tM")

            if KLVL >= 1:
                W("tensor", "sS")
                if ci == 0:
                    W("tensor", "dGC", SETUP_DW)
                for m in range(2):
                    for k in range(8):
                        E("tensor", lambda e, m=m, k=k: e.matmul(
                            pslh(0, m), BWT[:, k, m * 128:(m + 1) * 128],
                            STG[:, k, 1:C + 1], start=(k == 0), stop=(k == 7)),
                          "tM" if k == 7 else None)
                W("vector", "tM")
                W("vector", "sS", SS_LAM)
                if ci == 0:
                    W("vector", "gG", G_INIT)
                for m in range(2):
                    E("vector", lambda e, m=m: e.tensor_tensor_scan(
                        SCAN[:, m, :], LAM[:, m:m + 1].broadcast_to([128, C]),
                        pslh(0, m), SSMC[:, m:m + 1], ALU.mult, ALU.add))
                    E("vector", lambda e, m=m: e.tensor_copy(
                        SSMC[:, m:m + 1], SCAN[:, m, C - 1:C]))
                    E("vector", lambda e, m=m: e.tensor_copy(
                        HBF[:, m, :], SCAN[:, m, :]), "vV")
                for grp in range(2):
                    W("tensor", "vV")
                    for mt in range(4):
                        dt = grp * 4 + mt
                        for k in range(2):
                            E("tensor", lambda e, dt=dt, k=k, b=bank4[mt]: e.matmul(
                                b, CWT[:, k, dt * 128:(dt + 1) * 128], HBF[:, k, :],
                                start=(k == 0), stop=(k == 1)),
                              "tM" if k == 1 else None)
                    W("vector", "tM")
                    for mt in range(4):
                        dt = grp * 4 + mt
                        E("vector", lambda e, dt=dt, b=bank4[mt]:
                          e.scalar_tensor_tensor(
                              X1[:, dt, :], X1[:, dt, :], D1[:, dt:dt + 1], b,
                              ALU.mult, ALU.add), "vV")

        if KLVL >= 2:
            for ft in range(8):
                E("vector", lambda e, ft=ft: e.tensor_copy(
                    STG[:, ft, 1:C + 1], X1[:, ft, :]),
                  "vV" if ft == 7 else None)
            W("scalar", "vV")
            for ft in range(8):
                E("scalar", lambda e, ft=ft: e.activation(
                    SQ16v[:, ft, :], X1[:, ft, :], AF.Square),
                  "sS" if ft == 7 else None)
            W("tensor", "vV")
            W("tensor", "sS")
            for k in range(8):
                E("tensor", lambda e, k=k: e.matmul(
                    PSL[1][0:1, 0:512], ONESC[:], STG[:, k, 1:C + 1],
                    start=(k == 0), stop=(k == 7)))
            for k in range(8):
                E("tensor", lambda e, k=k: e.matmul(
                    PSL[1][0:1, 512:1024], ONESC[:], SQ16v[:, k, :],
                    start=(k == 0), stop=(k == 7)), "tM" if k == 7 else None)
            ln_rows(PSL[1])
            W("tensor", "vV")
            E("tensor", lambda e: e.matmul(PB[0][:], ONESR[:], RSTD16[:],
                                           start=True, stop=True))
            E("tensor", lambda e: e.matmul(PB[1][:], ONESR[:], SROW2[0:1, :],
                                           start=True, stop=True), "tM")
            W("vector", "tM")
            for ft in range(8):
                E("vector", lambda e, ft=ft: e.tensor_mul(
                    TMPF[:], X1[:, ft, :], PB[0][:]))
                E("vector", lambda e, ft=ft: e.tensor_add(
                    X1[:, ft, :], TMPF[:], PB[1][:]))
                E("vector", lambda e, ft=ft: e.tensor_copy(
                    Y[:, ft, 1:C + 1], X1[:, ft, :]),
                  "vV" if ft == 7 else None)
            VV_LN1 = cnt("vV")

        for li in range(2 if KLVL >= 3 else 0):
            if ci == 0 and li == 0:
                W("sync", "dPRO", PRO_ALL)

            W("gpsimd", "vV")
            G_LEAD1 = E("gpsimd", lambda e, li=li: e.tensor_copy(
                Y[:, 4:8, 0:1], C1[li][:]), "gG")
            W("vector", "gG", G_LEAD1)
            for ft in range(8):
                sl = slice(1, C + 1) if ft < 4 else slice(0, C + 1)
                E("vector", lambda e, ft=ft, sl=sl: e.tensor_copy(
                    STG[:, ft, sl], Y[:, ft, sl]), "vV" if ft == 7 else None)
            VV_STGA = cnt("vV")
            G_C1SAVE = E("gpsimd", lambda e, li=li: e.tensor_copy(
                C1[li][:], Y[:, 4:8, C:C + 1]), "gG")

            def qkv_lhsT(k, tt):
                off = tt * 128
                if k < 4:
                    return STG[:, k, 1 + off:1 + off + 128]
                return STG[:, k, off:off + 128]

            pass_banks = [
                [(pslh(0, 0), pslh(0, 1), PB[0][:]),
                 (pslh(1, 0), pslh(1, 1), PB[1][:])],
                [(PB[2][:], PB[3][:], PB[0][:])],
                [(pslh(0, 0), pslh(0, 1), PB[1][:])],
            ]
            pass_tts = [(0, 1), (2,), (3,)]
            vcopy_marks = {}
            kmul_marks = {}
            mm_base = {}

            def emit_post_group(tts_b, li=li, rb=rb, ci=ci):
                W("scalar", "tM")
                if ci > 0 or li > 0:
                    W("scalar", "gG")
                W("scalar", "vV")
                for n, (tt, (qb, kb, vb)) in enumerate(tts_b):
                    vcopy_marks[tt] = E(
                        "scalar", lambda e, li=li, tt=tt, vb=vb: e.copy(
                            VT[li][:, 4 + tt, :], vb), "sS")
                    E("scalar", lambda e, qb=qb, n=n: e.activation(
                        QSQv[:, 2 * n, :], qb, AF.Square), "sS")
                    E("scalar", lambda e, kb=kb, n=n: e.activation(
                        QSQv[:, 2 * n + 1, :], kb, AF.Square), "sS")
                W("vector", "sS")
                for n, (tt, banks) in enumerate(tts_b):
                    pp = tt % 2
                    E("vector", lambda e, pp=pp, n=n: e.reduce_sum(
                        NRM[:, pp, 0:8],
                        QSQv[:, 2 * n, :].rearrange("p (a b) -> p a b", a=8),
                        axis=AXX))
                    E("vector", lambda e, pp=pp, n=n: e.reduce_sum(
                        NRM[:, pp, 8:16],
                        QSQv[:, 2 * n + 1, :].rearrange(
                            "p (a b) -> p a b", a=8), axis=AXX))
                for n, (tt, banks) in enumerate(tts_b):
                    pp = tt % 2
                    E("vector", lambda e, pp=pp: e.tensor_scalar_max(
                        NRM[:, pp, :], NRM[:, pp, :], 1e-24))
                    E("vector", lambda e, pp=pp: e.reciprocal(
                        NRM[:, pp, :], NRM[:, pp, :]), "vV")
                W("scalar", "vV")
                for n, (tt, banks) in enumerate(tts_b):
                    pp = tt % 2
                    E("scalar", lambda e, pp=pp: e.activation(
                        RS[:, pp, :], NRM[:, pp, :], AF.Sqrt), "sS")
                W("vector", "sS")
                W("gpsimd", "sS")
                for n, (tt, (qb, kb, vb)) in enumerate(tts_b):
                    pp = tt % 2
                    for which, bank_, scl, tA, tB, dstv, eng, tsl in (
                            ("q", qb, QS8, 0, 1, QTOKv, "vector", 0),
                            ("k", kb, KS8, 2, 3, KTOKv, "gpsimd", 1)):
                        ns = 0 if which == "q" else 8
                        tq1 = TQ1v[:, tsl, :]
                        tr2 = TR2v[:, tsl, :]
                        tmpv = TMPF.ap() if eng == "vector" else TMPG
                        if which == "k":
                            W("vector", "gG")
                        mk = E("vector", lambda e, b=bank_, pp=pp, ns=ns,
                               tmpv=tmpv: e.tensor_mul(
                                   tmpv.rearrange("p (a b) -> p a b", a=8),
                                   b.rearrange("p (a b) -> p a b", a=8),
                                   RS[:, pp, ns:ns + 8].unsqueeze(2)
                                   .broadcast_to([128, 8, DH])), "vV")
                        if which == "k":
                            kmul_marks[tt] = mk
                            W(eng, "vV")
                        E(eng, lambda e, li=li, scl=scl, tq1=tq1, tmpv=tmpv:
                          e.tensor_mul(tq1, tmpv, scl[:, li, :]))
                        E(eng, lambda e, tA=tA, rb=rb, tt=tt, tq1=tq1,
                          tr2=tr2: e.tensor_mul(
                              tr2.rearrange("p (a b) -> p a b", a=8),
                              tq1.rearrange("p (a b) -> p a b", a=8),
                              RT[:, rb, tA, tt, :].unsqueeze(1)
                              .broadcast_to([128, 8, DH])))
                        q1h = tq1.rearrange("p (a b c) -> p a b c", a=8, b=2)
                        tmph = tmpv.rearrange("p (a b c) -> p a b c",
                                              a=8, b=2)
                        E(eng, lambda e, tB=tB, rb=rb, tt=tt, q1h=q1h,
                          tmph=tmph: e.tensor_mul(
                              tmph[:, :, 0, :], q1h[:, :, 1, :],
                              RT[:, rb, tB, tt, 0:32].unsqueeze(1)
                              .broadcast_to([128, 8, 32])))
                        E(eng, lambda e, tB=tB, rb=rb, tt=tt, q1h=q1h,
                          tmph=tmph: e.tensor_mul(
                              tmph[:, :, 1, :], q1h[:, :, 0, :],
                              RT[:, rb, tB, tt, 32:64].unsqueeze(1)
                              .broadcast_to([128, 8, 32])))
                        E(eng, lambda e, dv=dstv, tt=tt, tr2=tr2, tmpv=tmpv:
                          e.tensor_add(dv[:, tt, :], tr2, tmpv),
                          "vV" if eng == "vector" else "gG")

            for pas in range(3):
                tts = pass_tts[pas]
                for j in range(4):
                    buf = len(qkv_marks) % 2
                    if len(qkv_marks) >= 2:
                        W("sync", "tM", qkv_marks[-2])
                    dq = E("sync", lambda e, li=li, j=j, buf=buf: e.dma_start(
                        WQKVR[:, buf],
                        bass.AP(scr_wqkv, (li * 8 + 2 * j) * 128 * 1536,
                                [[1536, 128], [128 * 1536, 2], [1, 1536]])),
                      "dQK", 16)
                    W("tensor", "dQK", dq)
                    if j == 0:
                        if pas == 0:
                            WLAST("tensor", "scalar")
                            WLAST("tensor", "vector")
                            W("tensor", "vV", VV_STGA)
                        elif pas == 1:
                            W("tensor", "sS", vcopy_marks[0])
                        else:
                            W("tensor", "vV", kmul_marks[0])
                            W("tensor", "sS", vcopy_marks[1])
                    for kk in range(2):
                        k = 2 * j + kk
                        for ti, tt in enumerate(tts):
                            for third in range(3):
                                E("tensor",
                                  lambda e, b=pass_banks[pas][ti][third],
                                  buf=buf, kk=kk, k=k, tt=tt, third=third:
                                  e.matmul(
                                      b, qkv_lhsT(k, tt),
                                      WQKVR[:, buf, kk,
                                            third * 512:(third + 1) * 512],
                                      start=(k == 0), stop=(k == 7)),
                                  "tM" if (kk == 1 and ti == len(tts) - 1
                                           and third == 2) else None)
                    qkv_marks.append(cnt("tM"))
                emit_post_group(
                    [(tt, pass_banks[pas][ti]) for ti, tt in enumerate(tts)])

            W("tensor", "vV")
            W("tensor", "gG")
            if ci > 0 or li > 0:
                W("vector", "gG")
            for which, srcv, dpsum in (("q", QTOKv, 0), ("k", KTOKv, 1)):
                for ds in range(4):
                    b16 = pslb16(dpsum)[:, ds * 512:(ds + 1) * 512]
                    for tt in range(NT):
                        E("tensor", lambda e, b16=b16, sv=srcv, ds=ds,
                          tt=tt: e.transpose(
                              b16[:, tt * 128:(tt + 1) * 128],
                              sv[:, tt, ds * 128:(ds + 1) * 128], IDB[:]),
                          "tM" if tt == NT - 1 else None)
                    if ds % 2 == 1:
                        pair = ds // 2
                        W("vector", "tM")
                        if which == "q":
                            E("vector", lambda e, dpsum=dpsum, pair=pair:
                              e.tensor_copy(
                                  QFv[:, 2 * pair:2 * pair + 2, :]
                                  .rearrange("p a b -> p (a b)"),
                                  pslb16(dpsum)[:, pair * 1024:
                                                (pair + 1) * 1024]), "vV")
                        else:
                            E("vector", lambda e, li=li, dpsum=dpsum,
                              pair=pair: e.tensor_copy(
                                  KF[li][:, 2 * pair:2 * pair + 2,
                                         512:1024],
                                  pslb16(dpsum)[:, pair * 1024:
                                                (pair + 1) * 1024]
                                  .rearrange("p (a b) -> p a b", b=512)),
                              "vV")
            VV_QKF = cnt("vV")

            if KLVL >= 4:
                nun = NT * H
                baseL, baseM, baseE = cnt("aL"), cnt("aM"), cnt("aE")
                baseR, baseT2, baseV2 = cnt("aR"), cnt("aT2"), cnt("aV2")
                baseV2s = cnt("aV2s")
                baseA, baseO = cnt("aA"), cnt("aO")
                G_SHIFT_PRE = cnt("gG")

                def emit_L(i, li=li, ci=ci, VV_QKF=VV_QKF, baseE=baseE,
                           G_SHIFT_PRE=G_SHIFT_PRE):
                    tt, hh = i // H, i % H
                    lp = PSL[i % 2]
                    ds, hp = hh // 2, hh % 2
                    if i >= 2:
                        W("tensor", "aE", baseE + i - 1)
                    if i == 0:
                        W("tensor", "vV", VV_QKF)
                        W("tensor", "gG", G_SHIFT_PRE)
                    qT = QFv[:, ds, tt * 128:(tt + 1) * 128][
                        hp * 64:hp * 64 + 64]
                    kT = KF[li][:, ds, tt * 128:tt * 128 + 640][
                        hp * 64:hp * 64 + 64]
                    ma = MASKS[:, 0, :]
                    m0 = MCUT[:, tt, :] if ci == 0 else ma[:, 0:512]
                    E("tensor", lambda e, lp=lp, m0=m0: e.matmul(
                        lp[:, 0:512], IDB[:], m0, start=True, stop=False))
                    E("tensor", lambda e, lp=lp, ma=ma: e.matmul(
                        lp[:, 512:640], IDB[:], ma[:, 512:640], start=True,
                        stop=False))
                    E("tensor", lambda e, lp=lp, qT=qT, kT=kT: e.matmul(
                        lp[:, 0:512], qT, kT[:, 0:512], start=False,
                        stop=True))
                    E("tensor", lambda e, lp=lp, qT=qT, kT=kT: e.matmul(
                        lp[:, 512:640], qT, kT[:, 512:640], start=False,
                        stop=True), "aL")

                def emit_aT(i, baseR=baseR, baseV2=baseV2,
                            baseV2s=baseV2s):
                    W("tensor", "aR", baseR + i // 2 + 1)
                    if i >= 2:
                        j = i - 2
                        if j % 2 == 0:
                            W("tensor", "aV2", baseV2 + j // 2 + 1)
                        else:
                            W("tensor", "aV2s", baseV2s + j // 2 + 1)
                    atp = pbb16(i % 2)[:, 0:640]
                    for kq in range(5):
                        E("tensor", lambda e, atp=atp, i=i, kq=kq: e.transpose(
                            atp[:, kq * 128:(kq + 1) * 128],
                            AUv[:, i % 4, kq * 128:(kq + 1) * 128], IDB[:]),
                          "aT2" if kq == 4 else None)

                def emit_AV(i, li=li, baseV2=baseV2, baseV2s=baseV2s,
                            baseO=baseO):
                    tt, hh = i // H, i % H
                    if i % 2 == 0:
                        W("tensor", "aV2", baseV2 + i // 2 + 1)
                    else:
                        W("tensor", "aV2s", baseV2s + i // 2 + 1)
                    if hh == 0 and tt >= 2:
                        W("tensor", "aO", baseO + (tt - 1) * 4)
                    ob = PB[2 + tt % 2][:]
                    pc = (hh // 2) * 128
                    orow = (hh % 2) * 64
                    for kq in range(5):
                        E("tensor", lambda e, ob=ob, pc=pc, orow=orow, li=li,
                          tt=tt, kq=kq, hh=hh, i=i: e.matmul(
                              ob[orow:orow + 64, pc:pc + 128],
                              VT[li][:, tt + kq, hh * 64:(hh + 1) * 64],
                              ATv[:, i % 2, kq * 128:(kq + 1) * 128],
                              start=(kq == 0), stop=(kq == 4)),
                          "aA" if kq == 4 else None)

                emit_L(0)
                emit_L(1)
                for i in range(nun):
                    if i + 2 < nun:
                        emit_L(i + 2)
                    emit_aT(i)
                    emit_AV(i)
                for i in range(nun + 2):
                    if 1 <= i <= nun and (i - 1) % 2 == 1:
                        j = i - 1
                        sl = j % 4
                        W("vector", "aE", baseE + j + 1)
                        E("vector", lambda e, sl=sl: e.reciprocal(
                            RCP[:, sl - 1:sl + 1], DEN[:, sl - 1:sl + 1]))
                        E("vector", lambda e, j=j, sl=sl: e.tensor_mul(
                            AUv[:, j % 4 - 1:j % 4 + 1, :],
                            AUv[:, j % 4 - 1:j % 4 + 1, :],
                            RCP[:, sl - 1:sl + 1].unsqueeze(2)
                            .broadcast_to([128, 2, 640])), "aR")
                    if i >= 2 and (i - 2) % 2 == 0:
                        j = i - 2
                        W("vector", "aT2", baseT2 + j + 1)
                        E("vector", lambda e, j=j: e.tensor_copy(
                            ATv[:, j % 2, :], pbb16(j % 2)[:, 0:640]),
                          "aV2", chain=False)
                    if i < nun:
                        sl = i % 4
                        W("vector", "aL", baseL + i + 1)
                        E("vector", lambda e, i=i, sl=sl: e.reduce_max(
                            NM8[:, sl:sl + 1], PSL[i % 2][:, 0:640], axis=AXX,
                            negate=True), "aM", chain=False)
                oev = [0]

                def emit_oevac(baseA=baseA):
                    pr = oev[0]
                    tt, pq = pr // 4, pr % 4
                    W("scalar", "aA", baseA + (tt + 1) * H)
                    E("scalar", lambda e, tt=tt, pq=pq: e.copy(
                        OB[:, pq, tt * 128:(tt + 1) * 128],
                        PB[2 + tt % 2][:, pq * 128:pq * 128 + 128]), "aO")
                    oev[0] += 1

                for i in range(nun + 2):
                    sl = i % 4
                    if i >= 2 and (i - 2) % 2 == 1:
                        j = i - 2
                        W("scalar", "aT2", baseT2 + j + 1)
                        E("scalar", lambda e, j=j: e.copy(
                            ATv[:, j % 2, :], pbb16(j % 2)[:, 0:640]),
                          "aV2s")
                    if i >= nun:
                        continue
                    W("scalar", "aM", baseM + i + 1)
                    if i >= 4:
                        W("scalar", "aT2", baseT2 + i - 3)
                    E("scalar", lambda e, i=i, sl=sl: e.activation(
                        AUv[:, i % 4, :], PSL[i % 2][:, 0:640], AF.Exp,
                        bias=NM8[:, sl:sl + 1],
                        accum_out=DEN[:, sl:sl + 1]), "aE")
                    while (oev[0] < nun // 2
                           and (oev[0] // 4 + 1) * H + 2 <= i):
                        emit_oevac()
                while oev[0] < nun // 2:
                    emit_oevac()

                W("gpsimd", "aL", cnt("aL"))
                W("gpsimd", "aA", cnt("aA"))
                E("gpsimd", lambda e, li=li: e.tensor_copy(
                    KF[li][:, :, 0:512], KF[li][:, :, 512:1024]))
                E("gpsimd", lambda e, li=li: e.tensor_copy(
                    VT[li][:, 0:4, :], VT[li][:, 4:8, :]), "gG")
                if li == 1 and ci + 1 < nch:
                    W("gpsimd", "tM", p0_tm_mark[0])
                    emit_input_load("gpsimd", ci + 1)

            if KLVL >= 5:
                if wo_mm_mark[0] is not None:
                    W("sync", "tM", wo_mm_mark[0])
                E("sync", lambda e, li=li: e.dma_start(
                    WO.ap(), bass.AP(scr_wo, li * 4 * 128 * D,
                                     [[D, 128], [128 * D, 4], [1, D]])),
                  "dWO", 16)
                W("tensor", "dWO")
                W("tensor", "aO", cnt("aO"))
                W("vector", "gG", G_C1SAVE)
                wob = [pslh(0, 0), pslh(0, 1), pslh(1, 0), pslh(1, 1)]
                for grp in range(2):
                    if grp == 1:
                        W("tensor", "vV")
                    for mt in range(4):
                        dt = grp * 4 + mt
                        for k in range(4):
                            E("tensor", lambda e, b=wob[mt], dt=dt, k=k:
                              e.matmul(b, WO[:, k, dt * 128:(dt + 1) * 128],
                                       OB[:, k, :], start=(k == 0),
                                       stop=(k == 3)), "tM" if k == 3 else None)
                    if grp == 1:
                        wo_mm_mark[0] = cnt("tM")
                    W("vector", "tM")
                    for mt in range(4):
                        dt = grp * 4 + mt
                        E("vector", lambda e, dt=dt, b=wob[mt]: e.tensor_add(
                            Y[:, dt, 1:C + 1], Y[:, dt, 1:C + 1], b), "vV")
                VV_WO = cnt("vV")

                W("gpsimd", "vV", VV_WO)
                G_LEAD2 = E("gpsimd", lambda e, li=li: e.tensor_copy(
                    Y[:, 4:8, 0:1], C2[li][:]), "gG")
                W("gpsimd", "gG", G_LEAD2)
                G_C2SAVE = E("gpsimd", lambda e, li=li: e.tensor_copy(
                    C2[li][:], Y[:, 4:8, C:C + 1]), "gG")

            if KLVL >= 6:
                W("vector", "gG", G_LEAD2)
                for ft in range(8):
                    sl = slice(1, C + 1) if ft < 4 else slice(0, C + 1)
                    E("vector", lambda e, ft=ft, sl=sl: e.tensor_copy(
                        STG[:, ft, sl], Y[:, ft, sl]), "vV" if ft == 7 else None)
                VV_STGD = cnt("vV")

                def shv(k):
                    if k < 4:
                        return STG[:, k, 1:C + 1]
                    return STG[:, k, 0:C]

                W("scalar", "vV", VV_STGD)
                for ft in range(8):
                    E("scalar", lambda e, ft=ft, s=shv(ft): e.activation(
                        SQ16v[:, ft, :], s, AF.Square),
                      "sS" if ft == 7 else None)
                W("tensor", "vV", VV_STGD)
                W("tensor", "sS")
                for k in range(8):
                    E("tensor", lambda e, k=k, s=shv(k): e.matmul(
                        PSL[0][0:1, 0:512], ONESC[:], s,
                        start=(k == 0), stop=(k == 7)))
                for k in range(8):
                    E("tensor", lambda e, k=k: e.matmul(
                        PSL[0][0:1, 512:1024], ONESC[:], SQ16v[:, k, :],
                        start=(k == 0), stop=(k == 7)),
                      "tM" if k == 7 else None)
                ln_rows(PSL[0])
                W("tensor", "vV")
                xn_marks = []
                for dt in range(8 if KD >= 2 else 0):
                    rB = PB[0][:] if dt % 2 == 0 else PB[2][:]
                    sB = PB[1][:] if dt % 2 == 0 else PB[3][:]
                    if dt >= 2:
                        W("tensor", "vV", xn_marks[dt - 2])
                    E("tensor", lambda e, li=li, dt=dt, rB=rB: e.matmul(
                        rB, LGLB[0:1, li, dt * 128:(dt + 1) * 128],
                        RSTD16[:], start=True, stop=True))
                    E("tensor", lambda e, li=li, dt=dt, sB=sB: e.matmul(
                        sB, LGLB[:, li, dt * 128:(dt + 1) * 128],
                        SROW2[:], start=True, stop=True), "tM")
                    W("vector", "tM")
                    ysh = (Y[:, dt, 1:C + 1] if dt < 4 else Y[:, dt, 0:C])
                    E("vector", lambda e, ysh=ysh, rB=rB: e.tensor_mul(
                        TMPF[:], ysh, rB))
                    xn_marks.append(E("vector", lambda e, dt=dt, sB=sB:
                                      e.tensor_add(XNv[:, dt, :], TMPF[:],
                                                   sB), "vV"))
                VV_XN = cnt("vV")

                mb_T, mb_S, mb_V = cnt("mT"), cnt("mS"), cnt("mV")
                for m in range(NM if KD >= 3 else 0):
                    buf = m % 2
                    if m >= 2:
                        W("sync", "mT", mb_T + m - 1)
                    dv = E("sync", lambda e, li=li, m=m, buf=buf: e.dma_start(
                        W1R[:, buf],
                        bass.AP(scr_w1, (li * NM + m) * 128 * 2048,
                                [[2048, 128], [1, 2048]])), "dW1", 16)
                    W("tensor", "dW1", dv)
                    if m == 0:
                        W("tensor", "vV", VV_XN)
                    if m >= 2:
                        W("tensor", "mV", mb_V + m - 1)
                        W("tensor", "mS", mb_S + m - 1)
                    pa, pg = PB[buf][:], PB[2 + buf][:]
                    for k in range(8):
                        E("tensor", lambda e, pa=pa, buf=buf, k=k: e.matmul(
                            pa, W1R[:, buf, 0, k, :], XNv[:, k, :],
                            start=(k == 0), stop=(k == 7)))
                    for k in range(8):
                        E("tensor", lambda e, pg=pg, buf=buf, k=k: e.matmul(
                            pg, W1R[:, buf, 1, k, :], XNv[:, k, :],
                            start=(k == 0), stop=(k == 7)),
                          "mT" if k == 7 else None)
                    W("scalar", "mT")
                    E("scalar", lambda e, li=li, m=m, pg=pg, buf=buf:
                      e.activation(GEGv[:, buf, :], pg, AF.Gelu,
                                   bias=B1G[:, li, m:m + 1]), "mS")
                    W("vector", "mS")
                    E("vector", lambda e, li=li, m=m, pa=pa, buf=buf:
                      e.scalar_tensor_tensor(
                          HBv[:, m, :], pa, B1A[:, li, m:m + 1],
                          GEGv[:, buf, :], ALU.add, ALU.mult), "mV")
                MV_H = cnt("mV")

                w2_marks = {0: None, 1: None}
                for og in range(2 if KD >= 4 else 0):
                    for kg in range(11):
                        kn = 2
                        buf = (og * 11 + kg) % 2
                        if w2_marks[buf] is not None:
                            W("sync", "mT", w2_marks[buf])
                        dv = E("sync", lambda e, li=li, og=og, kg=kg,
                               buf=buf: e.dma_start(
                                   W2R[:, buf],
                                   bass.AP(scr_w2,
                                           ((li * 2 + og) * 11 + kg)
                                           * 128 * 1024,
                                           [[1024, 128], [1, 1024]])),
                               "dW2", 16)
                        W("tensor", "dW2", dv)
                        if kg == 0:
                            if og == 0:
                                W("tensor", "mV", MV_H)
                            else:
                                W("tensor", "vV")
                        for kk in range(kn):
                            kq = kg * 2 + kk
                            for mt in range(4):
                                E("tensor",
                                  lambda e, b=wob[mt], buf=buf, kk=kk,
                                  mt=mt, kq=kq: e.matmul(
                                      b,
                                      W2R[:, buf, kk, mt * 128:(mt + 1) * 128],
                                      HBv[:, kq, :], start=(kq == 0),
                                      stop=(kq == NM - 1)),
                                  "mT" if (kk == kn - 1 and mt == 3) else None)
                        w2_marks[buf] = cnt("mT")
                    W("vector", "mT")
                    W("vector", "gG", G_C2SAVE)
                    for mt in range(4):
                        dt = og * 4 + mt
                        E("vector", lambda e, li=li, dt=dt, b=wob[mt]:
                          e.scalar_tensor_tensor(
                              Y[:, dt, 1:C + 1], b,
                              B2[:, li, dt:dt + 1], Y[:, dt, 1:C + 1],
                              ALU.add, ALU.add), "vV")

        if KLVL >= 0:
            W("vector", "vV")
            for ft in range(8):
                E("vector", lambda e, ft=ft: e.tensor_add(
                    X1[:, ft, :], X1[:, ft, :], Y[:, ft, 1:C + 1]),
                  "vV" if ft == 7 else None)
            VV_W = cnt("vV")
            out_vv_marks = {}
            for tt in range(NT):
                obuf = tt % 2
                if tt == 0:
                    W("tensor", "vV", VV_W)
                    WLAST("tensor", "scalar")
                else:
                    W("tensor", "vV", out_vv_marks[tt - 1])
                for ft in range(8):
                    E("tensor", lambda e, tt=tt, ft=ft: e.transpose(
                        PSL[0][:, ft * 128:(ft + 1) * 128] if ft < 4 else
                        PSL[1][:, (ft - 4) * 128:(ft - 4 + 1) * 128],
                        X1[:, ft, tt * 128:(tt + 1) * 128], IDF[:]),
                      "tM" if ft == 7 else None)
                W("vector", "tM")
                if out_dma_marks[tt % 2] is not None:
                    W("vector", "dOUT0" if tt % 2 == 0 else "dOUT1",
                      out_dma_marks[tt % 2])
                E("vector", lambda e, ob=obuf: e.tensor_copy(
                    OUTB[:, ob, 0:512], PSL[0].ap()[:, 0:512]))
                E("vector", lambda e, ob=obuf: e.tensor_copy(
                    OUTB[:, ob, 512:1024], PSL[1].ap()[:, 0:512]))
                E("vector", lambda e, ob=obuf, tt=tt: e.reduce_sum(
                    SUMC[:, tt:tt + 1], OUTB[:, ob, :], axis=AXX), "vV")
                W("scalar", "vV")
                E("scalar", lambda e, ob=obuf, tt=tt: e.activation(
                    SQDUMP, OUTB[:, ob, :], AF.Square,
                    accum_out=SQC[:, tt:tt + 1]), "sS")
                W("vector", "sS")
                E("vector", lambda e, tt=tt: e.tensor_scalar_mul(
                    MUC[:, tt:tt + 1], SUMC[:, tt:tt + 1], 1.0 / 1024.0))
                E("vector", lambda e, tt=tt: e.tensor_scalar_mul(
                    E2C[:, tt:tt + 1], SQC[:, tt:tt + 1], 1.0 / 1024.0))
                E("vector", lambda e, tt=tt: e.tensor_mul(
                    VARC[:, tt:tt + 1], MUC[:, tt:tt + 1], MUC[:, tt:tt + 1]))
                E("vector", lambda e, tt=tt: e.tensor_sub(
                    VARC[:, tt:tt + 1], E2C[:, tt:tt + 1], VARC[:, tt:tt + 1]))
                E("vector", lambda e, tt=tt: e.tensor_scalar_add(
                    VARC[:, tt:tt + 1], VARC[:, tt:tt + 1], 1e-5))
                E("vector", lambda e, tt=tt: e.reciprocal(
                    RCPC[:, tt:tt + 1], VARC[:, tt:tt + 1]), "vV")
                W("scalar", "vV")
                E("scalar", lambda e, tt=tt: e.activation(
                    RSTDC[:, tt:tt + 1], RCPC[:, tt:tt + 1], AF.Sqrt), "sS")
                W("vector", "sS")
                E("vector", lambda e, tt=tt: e.scalar_tensor_tensor(
                    NMRC[:, tt:tt + 1], MUC[:, tt:tt + 1], -1.0,
                    RSTDC[:, tt:tt + 1], ALU.mult, ALU.mult), "vV")
                W("scalar", "vV")
                E("scalar", lambda e, ob=obuf, tt=tt: e.activation(
                    OUTB[:, ob, :], OUTB[:, ob, :], AF.Identity,
                    scale=RSTDC[:, tt:tt + 1], bias=NMRC[:, tt:tt + 1]), "sS")
                out_vv_marks[tt] = cnt("vV")
                W("sync", "sS")
                out_dma_marks[tt % 2] = E(
                    "sync", lambda e, ob=obuf, tt=tt, t0=t0: e.dma_start(
                        out_h.ap()[t0 + tt * 128:t0 + (tt + 1) * 128, :],
                        OUTB[:, ob, :]), "dOUT0" if tt % 2 == 0 else "dOUT1", 16)
    W("sync", "dOUT0")
    W("sync", "dOUT1")

    with nc.Block() as block:
        @block.sync
        def _(e):
            for f in ops["sync"]:
                f(e)

        @block.gpsimd
        def _(e):
            for f in ops["gpsimd"]:
                f(e)

        @block.tensor
        def _(e):
            for f in ops["tensor"]:
                f(e)

        @block.vector
        def _(e):
            for f in ops["vector"]:
                f(e)

        @block.scalar
        def _(e):
            for f in ops["scalar"]:
                f(e)

    return nc


def _prep(inputs):
    g = lambda k: np.asarray(inputs[k])
    f32 = lambda x: np.ascontiguousarray(x, dtype=np.float32)
    w1 = g("L0_ffn_w1")
    w2 = g("L0_ffn_w2")
    b1 = g("L0_ffn_b1")
    HID = 2730
    pad1 = HIDP - HID
    w1a = np.concatenate([w1[:, :, :HID],
                          np.zeros((2, D, pad1), np.float32)], axis=2)
    w1g = np.concatenate([w1[:, :, HID:],
                          np.zeros((2, D, pad1), np.float32)], axis=2)
    w2p = np.concatenate([w2, np.zeros((2, pad1, D), np.float32)], axis=1)
    b1a = np.concatenate([b1[:, :HID], np.zeros((2, pad1), np.float32)], 1)
    b1g = np.concatenate([b1[:, HID:], np.zeros((2, pad1), np.float32)], 1)
    qs = np.ascontiguousarray(np.broadcast_to(
        np.tile(g("L0_ffn_q_scale"), (1, H))[:, None, :],
        (2, 128, 512)).astype(np.float32))
    ks = np.ascontiguousarray(np.broadcast_to(
        np.tile(g("L0_ffn_k_scale"), (1, H))[:, None, :],
        (2, 128, 512)).astype(np.float32))
    lglb = np.stack([g("L0_ffn_ln_g"), g("L0_ffn_ln_b")], axis=0)
    return dict(
        bwT=f32(g("L0_attn_B_w").T),
        cwT=f32(g("L0_attn_C_w").T),
        llam=f32(g("L0_attn_log_lambda").reshape(2, 128).T),
        dvec=f32(g("L0_attn_D").reshape(8, 128).T),
        wqkv=f32(g("L0_ffn_wqkv")),
        wo=f32(g("L0_ffn_wo")),
        w1a=f32(w1a), w1g=f32(w1g), w2p=f32(w2p),
        b1a=f32(b1a.reshape(2, NM, 128).transpose(0, 2, 1)),
        b1g=f32(b1g.reshape(2, NM, 128).transpose(0, 2, 1)),
        b2=f32(g("L0_ffn_b2").reshape(2, 8, 128).transpose(0, 2, 1)),
        lglb=f32(lglb),
        qsc=qs, ksc=ks,
        idb=np.eye(128, dtype=BF),
        idf=np.eye(128, dtype=np.float32),
        onescol=np.ones((128, 1), dtype=BF),
        onesrow=np.ones((1, 128), dtype=BF),
        ones2r=np.ones((1, 512), dtype=BF),
        masks=host_masks(),
        mcut=host_mcut(),
        rtab=np.ascontiguousarray(np.stack(host_rotary_tables())),
    )


def kernel(**inputs):
    shared = _prep(inputs)
    u = np.asarray(inputs["L0_attn_u"])
    nc = build(T // C)
    in_maps = []
    for c in range(8):
        m = dict(shared)
        m["u"] = np.ascontiguousarray(u[c], dtype=np.float32)
        in_maps.append(m)
    res = run_bass_kernel_spmd(nc, in_maps, core_ids=list(range(8)))
    return np.stack([np.asarray(r["out"], dtype=np.float32)
                     for r in res.results])


# version 20
# speedup vs baseline: 1.0341x; 1.0341x over previous
import os
import numpy as np
import ml_dtypes
import concourse.bass as bass
import concourse.mybir as mybir
from concourse.bass_utils import run_bass_kernel_spmd
from contextlib import ExitStack

F32 = mybir.dt.float32
BF16 = mybir.dt.bfloat16
I8 = mybir.dt.int8
AF = mybir.ActivationFunctionType
ALU = mybir.AluOpType
AXX = mybir.AxisListType.X
BF = ml_dtypes.bfloat16

T = 8192
D = 1024
C = 512
NT = 4
H = 8
DH = 64
NM = 22
HIDP = 2816
NEG = -1.0e30


def host_rotary_tables():
    half = np.arange(0, DH, 2, dtype=np.float64)
    inv_freq = 1.0 / 10000.0 ** (half / DH)
    t = np.arange(T, dtype=np.float64)
    freqs = np.concatenate([t[:, None] * inv_freq[None, :]] * 2, axis=1)
    base = (half + 0.4 * DH) / (1.4 * DH)
    power = (t - T // 2) / 256.0
    scale = np.concatenate([base[None, :] ** power[:, None]] * 2, axis=1)
    cos, sin = np.cos(freqs), np.sin(freqs)
    qA, qB = cos * scale, sin * scale
    kA, kB = cos / scale, sin / scale
    qBs = qB.copy(); qBs[:, 0:32] *= -1.0
    kBs = kB.copy(); kBs[:, 0:32] *= -1.0
    return [x.astype(BF) for x in (qA, qBs, kA, kBs)]


def host_masks():
    p = np.arange(128)[:, None]
    c = np.arange(640)[None, :]
    band = (c >= p) & (c <= p + 512)
    return np.where(band, 0.0, NEG).astype(BF)


def host_mcut():
    p = np.arange(128)[:, None]
    c = np.arange(512)[None, :]
    band = (c >= p) & (c <= p + 512)
    cuts = [np.where(band & (c >= 512 - 128 * r), 0.0, NEG) for r in range(4)]
    return np.stack(cuts).astype(BF)


def build(nch):
    KLVL = int(os.environ.get("KLVL", "6"))
    KP0 = int(os.environ.get("KP0", "2"))
    KD = int(os.environ.get("KD", "4"))
    nc = bass.Bass("TRN2", target_bir_lowering=False)
    es = ExitStack()

    def din(name, shape, dt=F32):
        return nc.dram_tensor(name, shape, dt, kind="ExternalInput")

    u_h = din("u", [T, D])
    bwT_h = din("bwT", [D, 256])
    cwT_h = din("cwT", [256, D])
    llam_h = din("llam", [128, 2])
    d_h = din("dvec", [128, 8])
    wqkv_h = din("wqkv", [2, D, 1536])
    wo_h = din("wo", [2, 512, D])
    w1a_h = din("w1a", [2, D, HIDP])
    w1g_h = din("w1g", [2, D, HIDP])
    w2p_h = din("w2p", [2, HIDP, D])
    b1a_h = din("b1a", [2, 128, NM])
    b1g_h = din("b1g", [2, 128, NM])
    b2_h = din("b2", [2, 128, 8])
    lglb_h = din("lglb", [2, 2, D])
    qs_h = din("qsc", [2, 128, 512])
    ks_h = din("ksc", [2, 128, 512])
    idb_h = din("idb", [128, 128], BF16)
    idf_h = din("idf", [128, 128])
    onescol_h = din("onescol", [128, 1], BF16)
    onesrow_h = din("onesrow", [1, 128], BF16)
    ones2r_h = din("ones2r", [1, 512], BF16)
    masks_h = din("masks", [128, 640], BF16)
    mcut_h = din("mcut", [4, 128, 512], BF16)
    rtab_h = din("rtab", [4, T, DH], BF16)
    out_h = nc.dram_tensor("out", [T, D], F32, kind="ExternalOutput")

    scr_wqkv = nc.dram_tensor("scr_wqkv", [2, 8, 128, 1536], BF16)
    scr_wo = nc.dram_tensor("scr_wo", [2, 4, 128, D], BF16)
    scr_w1 = nc.dram_tensor("scr_w1", [2, NM, 128, 2, 8, 128], BF16)
    scr_w2 = nc.dram_tensor("scr_w2", [2, 2, 11, 128, 2, 512], BF16)

    A = nc.alloc_sbuf_tensor
    X1 = A("X1", [128, 8, C], F32)
    Y = A("Yb", [128, 8, C + 1], F32)
    STG = A("STG", [128, 8, C + 1], BF16)
    HBF = A("HBF", [128, 2, C], BF16)
    KF = [A(f"KF{l}", [128, 4, 1024], BF16) for l in range(2)]
    VT = [A(f"VT{l}", [128, 8, 512], BF16) for l in range(2)]
    OB = A("OB", [128, 4, C], BF16)
    TMPF = A("TMPF", [128, C], F32)
    USTG = A("USTG", [128, 4, D], F32)
    SCAN = USTG.ap()[:, 0].rearrange("p (a b) -> p a b", b=C)
    TMPG = USTG.ap()[:, 1, 0:C]
    OUTB = A("OUTB", [128, 2, D], F32)
    WQKVR = A("WQKVR", [128, 2, 2, 1536], BF16)
    WO = A("WOb", [128, 4, D], BF16)
    W1R = A("W1R", [128, 2, 2, 8, 128], BF16)
    W2R = A("W2R", [128, 2, 2, 512], BF16)
    BWT = A("BWT", [128, 8, 256], BF16)
    CWT = A("CWT", [128, 2, D], BF16)
    RT = A("RT", [128, 2, 4, 4, DH], BF16)
    QS8 = A("QS8", [128, 2, 512], BF16)
    KS8 = A("KS8", [128, 2, 512], BF16)
    MASKS = A("MASKS", [128, 1, 640], BF16)
    MCUT = A("MCUT", [128, 4, 512], BF16)
    IDB = A("IDB", [128, 128], BF16)
    IDF = A("IDF", [128, 128], F32)
    ONESC = A("ONESC", [128, 1], BF16)
    ONESR = A("ONESR", [1, 128], BF16)
    LGLB = A("LGLB", [2, 2, D], BF16)
    B1A = A("B1A", [128, 2, NM], F32)
    B1G = A("B1G", [128, 2, NM], F32)
    B2 = A("B2", [128, 2, 8], F32)
    DV = A("DV", [128, 8], F32)
    D1 = A("D1", [128, 8], F32)
    LLAM = A("LLAM", [128, 2], F32)
    LAM = A("LAM", [128, 2], F32)
    SSMC = A("SSMC", [128, 2], F32)
    C1 = [A(f"C1_{l}", [128, 4, 1], F32) for l in range(2)]
    C2 = [A(f"C2_{l}", [128, 4, 1], F32) for l in range(2)]
    NRM = A("NRM", [128, 2, 16], F32)
    RS = A("RS", [128, 2, 16], F32)
    MAXB = A("MAXB", [128, 4], F32)
    NM8 = A("NM8", [128, 4], F32)
    DEN = A("DEN", [128, 4], F32)
    RCP = A("RCP", [128, 4], F32)
    MU = A("MU", [1, 512], F32)
    E2 = A("E2", [1, 512], F32)
    VARR = A("VARR", [1, 512], F32)
    RCPV = A("RCPV", [1, 512], F32)
    RSTDF = A("RSTDF", [1, 512], F32)
    RSTD16 = A("RSTD16", [1, 512], BF16)
    SROW2 = A("SROW2", [2, 512], BF16)
    SUMC = A("SUMC", [128, 4], F32)
    SQC = A("SQC", [128, 4], F32)
    MUC = A("MUC", [128, 4], F32)
    E2C = A("E2C", [128, 4], F32)
    VARC = A("VARC", [128, 4], F32)
    RCPC = A("RCPC", [128, 4], F32)
    RSTDC = A("RSTDC", [128, 4], F32)
    NMRC = A("NMRC", [128, 4], F32)

    AR = A("AR", [128, 32768], I8)
    arv = AR.ap().bitcast(BF16)

    def arview(off, *shape):
        n = 1
        for s in shape:
            n *= s
        v = arv[:, off:off + n]
        if len(shape) == 2:
            v = v.rearrange("p (a b) -> p a b", b=shape[1])
        return v

    HBv = arview(0, NM, 512)
    XNv = arview(11264, 8, 512)
    GEGv = arview(15360, 2, 512)
    QFv = arview(0, 4, 512)
    QTOKv = arview(2048, 4, 512)
    KTOKv = arview(4096, 4, 512)
    AUv = arview(6144, 4, 640)
    ATv = arview(8704, 2, 640)
    TQ1v = arview(9984, 2, 512)
    TR2v = arview(11008, 2, 512)
    QSQv = arview(12032, 4, 512)
    SQ16v = arview(0, 8, 512)
    SQDUMP = arview(0, 1024)

    PSL = [nc.alloc_psum_tensor(f"PSL{i}", [128, 1024], F32) for i in range(2)]
    PB = [nc.alloc_psum_tensor(f"PB{i}", [128, 512], F32) for i in range(4)]

    def pslh(i, half):
        return PSL[i][:, half * 512:(half + 1) * 512]

    def pbb16(j):
        return PB[j].ap().bitcast(BF16)

    def pslb16(i):
        return PSL[i].ap().bitcast(BF16)

    sems = {}
    for name in ["dIN", "dSET", "dGC", "dPRO", "dOUT0", "dOUT1",
                 "dQK0", "dQK1", "dWO", "dW1a", "dW1b", "dW2a", "dW2b",
                 "tM", "vV", "sS", "gG",
                 "aL", "aM", "aE", "aR", "aT2", "aV2", "aA", "aO",
                 "mT", "mS", "mV", "oV", "oS", "oG", "aRc", "aV2s"]:
        sems[name] = [es.enter_context(nc.semaphore(name)), 0]

    ops = {k: [] for k in ["sync", "gpsimd", "tensor", "vector", "scalar"]}
    SELF = {"vector": "oV", "scalar": "oS", "gpsimd": "oG"}

    last_evt = {"vector": None, "scalar": None, "gpsimd": None}

    def E(eng, f, inc=None, n=1, dma=False, chain=True):
        self_sem = SELF.get(eng)
        chain = self_sem is not None and not dma and chain
        if chain and last_evt[eng] is not None:
            ph_, pv_ = last_evt[eng]
            ops[eng].append(lambda e, ph_=ph_, pv_=pv_: e.wait_ge(ph_, pv_))
        if inc is not None:
            sems[inc][1] += n
            h, v = sems[inc][0], sems[inc][1]
            if chain:
                last_evt[eng] = (h, v)
            ops[eng].append(lambda e, f=f, h=h, n=n: f(e).then_inc(h, n))
        elif chain:
            sems[self_sem][1] += 1
            h, v = sems[self_sem][0], sems[self_sem][1]
            last_evt[eng] = (h, v)
            ops[eng].append(lambda e, f=f, h=h: f(e).then_inc(h, 1))
        else:
            ops[eng].append(f)
        return sems[inc][1] if inc is not None else None

    def W(eng, name, val=None):
        v = sems[name][1] if val is None else val
        h = sems[name][0]
        ops[eng].append(lambda e, h=h, v=v: e.wait_ge(h, v))

    def cnt(name):
        return sems[name][1]

    def WLAST(eng, src_eng):
        evt = last_evt[src_eng]
        if evt is not None:
            h, v = evt
            ops[eng].append(lambda e, h=h, v=v: e.wait_ge(h, v))

    for dst, src in [(IDB[:], idb_h.ap()), (IDF[:], idf_h.ap()),
                     (ONESC[:], onescol_h.ap()), (ONESR[:], onesrow_h.ap()),
                     (SROW2[1:2, :], ones2r_h.ap()),
                     (MASKS.ap()[:, 0], masks_h.ap()),
                     (MCUT.ap(), mcut_h.ap().rearrange("m p c -> p m c")),
                     (DV[:], d_h.ap()), (LLAM[:], llam_h.ap()),
                     (B1A.ap(), b1a_h.ap().rearrange("l p m -> p l m")),
                     (B1G.ap(), b1g_h.ap().rearrange("l p m -> p l m")),
                     (B2.ap(), b2_h.ap().rearrange("l p m -> p l m"))]:
        E("sync", lambda e, d=dst, s=src: e.dma_start(d, s), "dSET", 16)
    SETUP_DIN = cnt("dSET")

    for dst, src in [
        (BWT.ap(), bass.AP(bwT_h, 0, [[256, 128], [128 * 256, 8], [1, 256]])),
        (CWT.ap(), bass.AP(cwT_h, 0,
                           [[1024, 128], [128 * 1024, 2], [1, 1024]])),
        (QS8.ap(), qs_h.ap().rearrange("l p c -> p l c")),
        (KS8.ap(), ks_h.ap().rearrange("l p c -> p l c")),
        (LGLB.ap(), lglb_h.ap()),
    ]:
        E("gpsimd", lambda e, d=dst, s=src: e.dma_start(d, s), "dGC", 16,
          dma=True)
    SETUP_DW = cnt("dGC")

    zb = [SSMC[:], C1[0][:], C1[1][:], C2[0][:], C2[1][:],
          KF[0][:, :, 0:512], KF[1][:, :, 0:512],
          VT[0][:, 0:4, :], VT[1][:, 0:4, :]]
    for i, buf in enumerate(zb):
        E("gpsimd", lambda e, b=buf: e.memset(b, 0.0),
          "gG" if i == len(zb) - 1 else None)
    G_INIT = cnt("gG")

    W("vector", "dSET", SETUP_DIN)
    E("vector", lambda e: e.tensor_scalar_add(D1[:], DV[:], 1.0), "vV")
    W("vector", "dGC", SETUP_DW)
    E("vector", lambda e: e.tensor_scalar_mul(
        QS8.ap().rearrange("p a b -> p (a b)"),
        QS8.ap().rearrange("p a b -> p (a b)"), 8.0), "vV")
    W("scalar", "dSET", SETUP_DIN)
    SS_LAM = E("scalar", lambda e: e.activation(LAM[:], LLAM[:], AF.Sigmoid),
               "sS")

    for l in range(2):
        for k in range(8):
            E("gpsimd", lambda e, l=l, k=k: e.dma_start(
                scr_wqkv.ap()[l, k],
                wqkv_h.ap()[l, k * 128:(k + 1) * 128, :]), "dPRO", 16,
              dma=True)
        for k in range(4):
            E("gpsimd", lambda e, l=l, k=k: e.dma_start(
                scr_wo.ap()[l, k],
                wo_h.ap()[l, k * 128:(k + 1) * 128, :]), "dPRO", 16,
              dma=True)
        for m in range(NM):
            for s, wsrc in ((0, w1a_h), (1, w1g_h)):
                E("gpsimd", lambda e, l=l, m=m, s=s, wsrc=wsrc: e.dma_start(
                    scr_w1.ap()[l, m][:, s],
                    bass.AP(wsrc, l * D * HIDP + m * 128,
                            [[HIDP, 128], [128 * HIDP, 8], [1, 128]])),
                  "dPRO", 16, dma=True)
        for og in range(2):
            for kq in range(NM):
                E("gpsimd", lambda e, l=l, og=og, kq=kq: e.dma_start(
                    scr_w2.ap()[l, og, kq // 2][:, kq % 2],
                    bass.AP(w2p_h, l * HIDP * D + kq * 128 * D + og * 512,
                            [[D, 128], [1, 512]])), "dPRO", 16, dma=True)
    PRO_ALL = cnt("dPRO")

    bank4 = [PB[0][:], PB[1][:], PB[2][:], PB[3][:]]

    def ln_rows(srct):
        W("vector", "tM")
        E("vector", lambda e, srct=srct: e.tensor_scalar_mul(
            MU[:], srct[0:1, 0:512], 1.0 / 1024.0))
        E("vector", lambda e, srct=srct: e.tensor_scalar_mul(
            E2[:], srct[0:1, 512:1024], 1.0 / 1024.0))
        E("vector", lambda e: e.tensor_mul(VARR[:], MU[:], MU[:]))
        E("vector", lambda e: e.tensor_sub(VARR[:], E2[:], VARR[:]))
        E("vector", lambda e: e.tensor_scalar_add(VARR[:], VARR[:], 1e-5))
        E("vector", lambda e: e.reciprocal(RCPV[:], VARR[:]), "vV")
        W("scalar", "vV")
        E("scalar", lambda e: e.activation(RSTDF[:], RCPV[:], AF.Sqrt))
        E("scalar", lambda e: e.activation(RSTD16[:], RCPV[:], AF.Sqrt), "sS")
        W("vector", "sS")
        E("vector", lambda e: e.scalar_tensor_tensor(
            SROW2[0:1, :], MU[:], -1.0, RSTDF[:], ALU.mult, ALU.mult), "vV")

    wo_mm_mark = [None]
    p0_tm_mark = [None]
    out_dma_marks = {0: None, 1: None}
    qkv_marks = []

    def emit_input_load(eng, cit):
        E(eng, lambda e, t0c=cit * C: e.dma_start(
            USTG.ap(), bass.AP(u_h, t0c * D,
                               [[D, 128], [128 * D, 4], [1, D]])),
          "dIN", 16, dma=True)
        for mtb in range(4):
            E(eng, lambda e, rbt=cit % 2, t0c=cit * C, mtb=mtb: e.dma_start(
                RT.ap()[:, rbt, mtb],
                bass.AP(rtab_h, mtb * T * DH + t0c * DH,
                        [[DH, 128], [128 * DH, 4], [1, DH]])),
              "dIN", 16, dma=True)

    for ci in range(nch):
        t0 = ci * C
        rb = ci % 2

        if KLVL >= -1:
            if ci == 0:
                emit_input_load("sync", 0)
            W("tensor", "dIN", 80 * (ci + 1))
            if ci == 0:
                W("tensor", "dSET", SETUP_DIN)
            else:
                WLAST("tensor", "vector")
                WLAST("tensor", "scalar")
            p0_vv, p0_ss = [], []
            for ft in range(8 if KP0 >= 2 else 0):
                bank = bank4[ft % 4]
                if ft >= 4:
                    W("tensor", "vV", p0_vv[ft - 4])
                    W("tensor", "sS", p0_ss[ft - 4])
                for tt in range(NT):
                    E("tensor", lambda e, b=bank, ft=ft, tt=tt: e.transpose(
                        b[:, tt * 128:(tt + 1) * 128],
                        USTG[:, tt, ft * 128:(ft + 1) * 128], IDF[:]),
                      "tM" if tt == NT - 1 else None)
                W("vector", "tM")
                p0_vv.append(E("vector", lambda e, b=bank, ft=ft: e.tensor_copy(
                    X1[:, ft, :], b), "vV"))
                W("scalar", "vV", p0_vv[-1])
                p0_ss.append(E("scalar", lambda e, b=bank, ft=ft: e.copy(
                    STG[:, ft, 1:C + 1], b), "sS"))
            p0_tm_mark[0] = cnt("tM")

            if KLVL >= 1:
                W("tensor", "sS")
                if ci == 0:
                    W("tensor", "dGC", SETUP_DW)
                for m in range(2):
                    for k in range(8):
                        E("tensor", lambda e, m=m, k=k: e.matmul(
                            pslh(0, m), BWT[:, k, m * 128:(m + 1) * 128],
                            STG[:, k, 1:C + 1], start=(k == 0), stop=(k == 7)),
                          "tM" if k == 7 else None)
                W("vector", "tM")
                W("vector", "sS", SS_LAM)
                if ci == 0:
                    W("vector", "gG", G_INIT)
                for m in range(2):
                    E("vector", lambda e, m=m: e.tensor_tensor_scan(
                        SCAN[:, m, :], LAM[:, m:m + 1].broadcast_to([128, C]),
                        pslh(0, m), SSMC[:, m:m + 1], ALU.mult, ALU.add))
                    E("vector", lambda e, m=m: e.tensor_copy(
                        SSMC[:, m:m + 1], SCAN[:, m, C - 1:C]))
                    E("vector", lambda e, m=m: e.tensor_copy(
                        HBF[:, m, :], SCAN[:, m, :]), "vV")
                for grp in range(2):
                    W("tensor", "vV")
                    for mt in range(4):
                        dt = grp * 4 + mt
                        for k in range(2):
                            E("tensor", lambda e, dt=dt, k=k, b=bank4[mt]: e.matmul(
                                b, CWT[:, k, dt * 128:(dt + 1) * 128], HBF[:, k, :],
                                start=(k == 0), stop=(k == 1)),
                              "tM" if k == 1 else None)
                    W("vector", "tM")
                    for mt in range(4):
                        dt = grp * 4 + mt
                        E("vector", lambda e, dt=dt, b=bank4[mt]:
                          e.scalar_tensor_tensor(
                              X1[:, dt, :], X1[:, dt, :], D1[:, dt:dt + 1], b,
                              ALU.mult, ALU.add), "vV")

        if KLVL >= 2:
            for ft in range(8):
                E("vector", lambda e, ft=ft: e.tensor_copy(
                    STG[:, ft, 1:C + 1], X1[:, ft, :]),
                  "vV" if ft == 7 else None)
            W("scalar", "vV")
            for ft in range(8):
                E("scalar", lambda e, ft=ft: e.activation(
                    SQ16v[:, ft, :], X1[:, ft, :], AF.Square),
                  "sS" if ft == 7 else None)
            W("tensor", "vV")
            W("tensor", "sS")
            for k in range(8):
                E("tensor", lambda e, k=k: e.matmul(
                    PSL[1][0:1, 0:512], ONESC[:], STG[:, k, 1:C + 1],
                    start=(k == 0), stop=(k == 7)))
            for k in range(8):
                E("tensor", lambda e, k=k: e.matmul(
                    PSL[1][0:1, 512:1024], ONESC[:], SQ16v[:, k, :],
                    start=(k == 0), stop=(k == 7)), "tM" if k == 7 else None)
            ln_rows(PSL[1])
            W("tensor", "vV")
            E("tensor", lambda e: e.matmul(PB[0][:], ONESR[:], RSTD16[:],
                                           start=True, stop=True))
            E("tensor", lambda e: e.matmul(PB[1][:], ONESR[:], SROW2[0:1, :],
                                           start=True, stop=True), "tM")
            W("vector", "tM")
            for ft in range(8):
                E("vector", lambda e, ft=ft: e.tensor_mul(
                    TMPF[:], X1[:, ft, :], PB[0][:]))
                E("vector", lambda e, ft=ft: e.tensor_add(
                    X1[:, ft, :], TMPF[:], PB[1][:]))
                E("vector", lambda e, ft=ft: e.tensor_copy(
                    Y[:, ft, 1:C + 1], X1[:, ft, :]),
                  "vV" if ft == 7 else None)
            VV_LN1 = cnt("vV")

        for li in range(2 if KLVL >= 3 else 0):
            if ci == 0 and li == 0:
                W("sync", "dPRO", PRO_ALL)

            W("gpsimd", "vV")
            G_LEAD1 = E("gpsimd", lambda e, li=li: e.tensor_copy(
                Y[:, 4:8, 0:1], C1[li][:]), "gG")
            W("vector", "gG", G_LEAD1)
            for ft in range(8):
                sl = slice(1, C + 1) if ft < 4 else slice(0, C + 1)
                E("vector", lambda e, ft=ft, sl=sl: e.tensor_copy(
                    STG[:, ft, sl], Y[:, ft, sl]), "vV" if ft == 7 else None)
            VV_STGA = cnt("vV")
            G_C1SAVE = E("gpsimd", lambda e, li=li: e.tensor_copy(
                C1[li][:], Y[:, 4:8, C:C + 1]), "gG")

            def qkv_lhsT(k, tt):
                off = tt * 128
                if k < 4:
                    return STG[:, k, 1 + off:1 + off + 128]
                return STG[:, k, off:off + 128]

            pass_banks = [
                [(pslh(0, 0), pslh(0, 1), PB[0][:]),
                 (pslh(1, 0), pslh(1, 1), PB[1][:])],
                [(PB[2][:], PB[3][:], PB[0][:])],
                [(pslh(0, 0), pslh(0, 1), PB[1][:])],
            ]
            pass_tts = [(0, 1), (2,), (3,)]
            vcopy_marks = {}
            kmul_marks = {}
            mm_base = {}

            def emit_post_group(tts_b, li=li, rb=rb, ci=ci):
                W("scalar", "tM")
                if ci > 0 or li > 0:
                    W("scalar", "gG")
                W("scalar", "vV")
                for n, (tt, (qb, kb, vb)) in enumerate(tts_b):
                    vcopy_marks[tt] = E(
                        "scalar", lambda e, li=li, tt=tt, vb=vb: e.copy(
                            VT[li][:, 4 + tt, :], vb), "sS")
                    E("scalar", lambda e, qb=qb, n=n: e.activation(
                        QSQv[:, 2 * n, :], qb, AF.Square), "sS")
                    E("scalar", lambda e, kb=kb, n=n: e.activation(
                        QSQv[:, 2 * n + 1, :], kb, AF.Square), "sS")
                W("vector", "sS")
                for n, (tt, banks) in enumerate(tts_b):
                    pp = tt % 2
                    E("vector", lambda e, pp=pp, n=n: e.reduce_sum(
                        NRM[:, pp, 0:8],
                        QSQv[:, 2 * n, :].rearrange("p (a b) -> p a b", a=8),
                        axis=AXX))
                    E("vector", lambda e, pp=pp, n=n: e.reduce_sum(
                        NRM[:, pp, 8:16],
                        QSQv[:, 2 * n + 1, :].rearrange(
                            "p (a b) -> p a b", a=8), axis=AXX))
                for n, (tt, banks) in enumerate(tts_b):
                    pp = tt % 2
                    E("vector", lambda e, pp=pp: e.tensor_scalar_max(
                        NRM[:, pp, :], NRM[:, pp, :], 1e-24))
                    E("vector", lambda e, pp=pp: e.reciprocal(
                        NRM[:, pp, :], NRM[:, pp, :]), "vV")
                W("scalar", "vV")
                for n, (tt, banks) in enumerate(tts_b):
                    pp = tt % 2
                    E("scalar", lambda e, pp=pp: e.activation(
                        RS[:, pp, :], NRM[:, pp, :], AF.Sqrt), "sS")
                W("vector", "sS")
                W("gpsimd", "sS")
                for n, (tt, (qb, kb, vb)) in enumerate(tts_b):
                    pp = tt % 2
                    for which, bank_, scl, tA, tB, dstv, eng, tsl in (
                            ("q", qb, QS8, 0, 1, QTOKv, "vector", 0),
                            ("k", kb, KS8, 2, 3, KTOKv, "gpsimd", 1)):
                        ns = 0 if which == "q" else 8
                        tq1 = TQ1v[:, tsl, :]
                        tr2 = TR2v[:, tsl, :]
                        tmpv = TMPF.ap() if eng == "vector" else TMPG
                        if which == "k":
                            W("vector", "gG")
                        mk = E("vector", lambda e, b=bank_, pp=pp, ns=ns,
                               tmpv=tmpv: e.tensor_mul(
                                   tmpv.rearrange("p (a b) -> p a b", a=8),
                                   b.rearrange("p (a b) -> p a b", a=8),
                                   RS[:, pp, ns:ns + 8].unsqueeze(2)
                                   .broadcast_to([128, 8, DH])), "vV")
                        if which == "k":
                            kmul_marks[tt] = mk
                            W(eng, "vV")
                        E(eng, lambda e, li=li, scl=scl, tq1=tq1, tmpv=tmpv:
                          e.tensor_mul(tq1, tmpv, scl[:, li, :]))
                        E(eng, lambda e, tA=tA, rb=rb, tt=tt, tq1=tq1,
                          tr2=tr2: e.tensor_mul(
                              tr2.rearrange("p (a b) -> p a b", a=8),
                              tq1.rearrange("p (a b) -> p a b", a=8),
                              RT[:, rb, tA, tt, :].unsqueeze(1)
                              .broadcast_to([128, 8, DH])))
                        q1h = tq1.rearrange("p (a b c) -> p a b c", a=8, b=2)
                        tmph = tmpv.rearrange("p (a b c) -> p a b c",
                                              a=8, b=2)
                        E(eng, lambda e, tB=tB, rb=rb, tt=tt, q1h=q1h,
                          tmph=tmph: e.tensor_mul(
                              tmph[:, :, 0, :], q1h[:, :, 1, :],
                              RT[:, rb, tB, tt, 0:32].unsqueeze(1)
                              .broadcast_to([128, 8, 32])))
                        E(eng, lambda e, tB=tB, rb=rb, tt=tt, q1h=q1h,
                          tmph=tmph: e.tensor_mul(
                              tmph[:, :, 1, :], q1h[:, :, 0, :],
                              RT[:, rb, tB, tt, 32:64].unsqueeze(1)
                              .broadcast_to([128, 8, 32])))
                        E(eng, lambda e, dv=dstv, tt=tt, tr2=tr2, tmpv=tmpv:
                          e.tensor_add(dv[:, tt, :], tr2, tmpv),
                          "vV" if eng == "vector" else "gG")

            for pas in range(3):
                tts = pass_tts[pas]
                for j in range(4):
                    buf = len(qkv_marks) % 2
                    if len(qkv_marks) >= 2:
                        W("sync", "tM", qkv_marks[-2])
                    dq = E("sync", lambda e, li=li, j=j, buf=buf: e.dma_start(
                        WQKVR[:, buf],
                        bass.AP(scr_wqkv, (li * 8 + 2 * j) * 128 * 1536,
                                [[1536, 128], [128 * 1536, 2], [1, 1536]])),
                      f"dQK{buf}", 16)
                    W("tensor", f"dQK{buf}", dq)
                    if j == 0:
                        if pas == 0:
                            WLAST("tensor", "scalar")
                            WLAST("tensor", "vector")
                            W("tensor", "vV", VV_STGA)
                        elif pas == 1:
                            W("tensor", "sS", vcopy_marks[0])
                        else:
                            W("tensor", "vV", kmul_marks[0])
                            W("tensor", "sS", vcopy_marks[1])
                    for kk in range(2):
                        k = 2 * j + kk
                        for ti, tt in enumerate(tts):
                            for third in range(3):
                                E("tensor",
                                  lambda e, b=pass_banks[pas][ti][third],
                                  buf=buf, kk=kk, k=k, tt=tt, third=third:
                                  e.matmul(
                                      b, qkv_lhsT(k, tt),
                                      WQKVR[:, buf, kk,
                                            third * 512:(third + 1) * 512],
                                      start=(k == 0), stop=(k == 7)),
                                  "tM" if (kk == 1 and ti == len(tts) - 1
                                           and third == 2) else None)
                    qkv_marks.append(cnt("tM"))
                emit_post_group(
                    [(tt, pass_banks[pas][ti]) for ti, tt in enumerate(tts)])

            W("tensor", "vV")
            W("tensor", "gG")
            if ci > 0 or li > 0:
                W("vector", "gG")
            for which, srcv, dpsum in (("q", QTOKv, 0), ("k", KTOKv, 1)):
                for ds in range(4):
                    b16 = pslb16(dpsum)[:, ds * 512:(ds + 1) * 512]
                    for tt in range(NT):
                        E("tensor", lambda e, b16=b16, sv=srcv, ds=ds,
                          tt=tt: e.transpose(
                              b16[:, tt * 128:(tt + 1) * 128],
                              sv[:, tt, ds * 128:(ds + 1) * 128], IDB[:]),
                          "tM" if tt == NT - 1 else None)
                    if ds % 2 == 1:
                        pair = ds // 2
                        W("vector", "tM")
                        if which == "q":
                            E("vector", lambda e, dpsum=dpsum, pair=pair:
                              e.tensor_copy(
                                  QFv[:, 2 * pair:2 * pair + 2, :]
                                  .rearrange("p a b -> p (a b)"),
                                  pslb16(dpsum)[:, pair * 1024:
                                                (pair + 1) * 1024]), "vV")
                        else:
                            E("vector", lambda e, li=li, dpsum=dpsum,
                              pair=pair: e.tensor_copy(
                                  KF[li][:, 2 * pair:2 * pair + 2,
                                         512:1024],
                                  pslb16(dpsum)[:, pair * 1024:
                                                (pair + 1) * 1024]
                                  .rearrange("p (a b) -> p a b", b=512)),
                              "vV")
            VV_QKF = cnt("vV")

            if KLVL >= 4:
                nun = NT * H
                baseL, baseM, baseE = cnt("aL"), cnt("aM"), cnt("aE")
                baseR, baseT2, baseV2 = cnt("aR"), cnt("aT2"), cnt("aV2")
                baseV2s = cnt("aV2s")
                baseA, baseO = cnt("aA"), cnt("aO")
                G_SHIFT_PRE = cnt("gG")

                def emit_L(i, li=li, ci=ci, VV_QKF=VV_QKF, baseE=baseE,
                           G_SHIFT_PRE=G_SHIFT_PRE):
                    tt, hh = i // H, i % H
                    lp = PSL[i % 2]
                    ds, hp = hh // 2, hh % 2
                    if i >= 2:
                        W("tensor", "aE", baseE + i - 1)
                    if i == 0:
                        W("tensor", "vV", VV_QKF)
                        W("tensor", "gG", G_SHIFT_PRE)
                    qT = QFv[:, ds, tt * 128:(tt + 1) * 128][
                        hp * 64:hp * 64 + 64]
                    kT = KF[li][:, ds, tt * 128:tt * 128 + 640][
                        hp * 64:hp * 64 + 64]
                    ma = MASKS[:, 0, :]
                    m0 = MCUT[:, tt, :] if ci == 0 else ma[:, 0:512]
                    E("tensor", lambda e, lp=lp, m0=m0: e.matmul(
                        lp[:, 0:512], IDB[:], m0, start=True, stop=False))
                    E("tensor", lambda e, lp=lp, ma=ma: e.matmul(
                        lp[:, 512:640], IDB[:], ma[:, 512:640], start=True,
                        stop=False))
                    E("tensor", lambda e, lp=lp, qT=qT, kT=kT: e.matmul(
                        lp[:, 0:512], qT, kT[:, 0:512], start=False,
                        stop=True))
                    E("tensor", lambda e, lp=lp, qT=qT, kT=kT: e.matmul(
                        lp[:, 512:640], qT, kT[:, 512:640], start=False,
                        stop=True), "aL")

                def emit_aT(i, baseR=baseR, baseV2=baseV2,
                            baseV2s=baseV2s):
                    W("tensor", "aR", baseR + i + 1)
                    if i >= 2:
                        j = i - 2
                        if j % 2 == 0:
                            W("tensor", "aV2", baseV2 + j // 2 + 1)
                        else:
                            W("tensor", "aV2s", baseV2s + j // 2 + 1)
                    atp = pbb16(i % 2)[:, 0:640]
                    for kq in range(5):
                        E("tensor", lambda e, atp=atp, i=i, kq=kq: e.transpose(
                            atp[:, kq * 128:(kq + 1) * 128],
                            AUv[:, i % 4, kq * 128:(kq + 1) * 128], IDB[:]),
                          "aT2" if kq == 4 else None)

                def emit_AV(i, li=li, baseV2=baseV2, baseV2s=baseV2s,
                            baseO=baseO):
                    tt, hh = i // H, i % H
                    if i % 2 == 0:
                        W("tensor", "aV2", baseV2 + i // 2 + 1)
                    else:
                        W("tensor", "aV2s", baseV2s + i // 2 + 1)
                    if hh == 0 and tt >= 2:
                        W("tensor", "aO", baseO + (tt - 1) * 4)
                    ob = PB[2 + tt % 2][:]
                    pc = (hh // 2) * 128
                    orow = (hh % 2) * 64
                    for kq in range(5):
                        E("tensor", lambda e, ob=ob, pc=pc, orow=orow, li=li,
                          tt=tt, kq=kq, hh=hh, i=i: e.matmul(
                              ob[orow:orow + 64, pc:pc + 128],
                              VT[li][:, tt + kq, hh * 64:(hh + 1) * 64],
                              ATv[:, i % 2, kq * 128:(kq + 1) * 128],
                              start=(kq == 0), stop=(kq == 4)),
                          "aA" if kq == 4 else None)

                emit_L(0)
                emit_L(1)
                for i in range(nun):
                    if i + 2 < nun:
                        emit_L(i + 2)
                    emit_aT(i)
                    emit_AV(i)
                for i in range(nun + 2):
                    if i >= 2 and (i - 2) % 2 == 0:
                        j = i - 2
                        W("vector", "aT2", baseT2 + j + 1)
                        E("vector", lambda e, j=j: e.tensor_copy(
                            ATv[:, j % 2, :], pbb16(j % 2)[:, 0:640]),
                          "aV2", chain=False)
                    if 1 <= i <= nun:
                        j = i - 1
                        sl = j % 4
                        W("vector", "aE", baseE + j + 1)
                        E("vector", lambda e, sl=sl: e.reciprocal(
                            RCP[:, sl:sl + 1], DEN[:, sl:sl + 1]))
                        E("vector", lambda e, j=j, sl=sl: e.tensor_scalar_mul(
                            AUv[:, j % 4, :], AUv[:, j % 4, :],
                            RCP[:, sl:sl + 1]), "aR")
                    if i < nun:
                        sl = i % 4
                        W("vector", "aL", baseL + i + 1)
                        E("vector", lambda e, i=i, sl=sl: e.reduce_max(
                            NM8[:, sl:sl + 1], PSL[i % 2][:, 0:640], axis=AXX,
                            negate=True), "aM", chain=False)
                oev = [0]

                def emit_oevac(baseA=baseA):
                    pr = oev[0]
                    tt, pq = pr // 4, pr % 4
                    W("scalar", "aA", baseA + (tt + 1) * H)
                    E("scalar", lambda e, tt=tt, pq=pq: e.copy(
                        OB[:, pq, tt * 128:(tt + 1) * 128],
                        PB[2 + tt % 2][:, pq * 128:pq * 128 + 128]), "aO")
                    oev[0] += 1

                for i in range(nun + 2):
                    sl = i % 4
                    if i >= 2 and (i - 2) % 2 == 1:
                        j = i - 2
                        W("scalar", "aT2", baseT2 + j + 1)
                        E("scalar", lambda e, j=j: e.copy(
                            ATv[:, j % 2, :], pbb16(j % 2)[:, 0:640]),
                          "aV2s")
                    if i >= nun:
                        continue
                    W("scalar", "aM", baseM + i + 1)
                    if i >= 4:
                        W("scalar", "aT2", baseT2 + i - 3)
                    E("scalar", lambda e, i=i, sl=sl: e.activation(
                        AUv[:, i % 4, :], PSL[i % 2][:, 0:640], AF.Exp,
                        bias=NM8[:, sl:sl + 1],
                        accum_out=DEN[:, sl:sl + 1]), "aE")
                    while (oev[0] < nun // 2
                           and (oev[0] // 4 + 1) * H + 2 <= i):
                        emit_oevac()
                while oev[0] < nun // 2:
                    emit_oevac()

                W("gpsimd", "aL", cnt("aL"))
                W("gpsimd", "aA", cnt("aA"))
                E("gpsimd", lambda e, li=li: e.tensor_copy(
                    KF[li][:, :, 0:512], KF[li][:, :, 512:1024]))
                E("gpsimd", lambda e, li=li: e.tensor_copy(
                    VT[li][:, 0:4, :], VT[li][:, 4:8, :]), "gG")
                if li == 1 and ci + 1 < nch:
                    W("gpsimd", "tM", p0_tm_mark[0])
                    emit_input_load("gpsimd", ci + 1)

            if KLVL >= 5:
                if wo_mm_mark[0] is not None:
                    W("sync", "tM", wo_mm_mark[0])
                E("sync", lambda e, li=li: e.dma_start(
                    WO.ap(), bass.AP(scr_wo, li * 4 * 128 * D,
                                     [[D, 128], [128 * D, 4], [1, D]])),
                  "dWO", 16)
                W("tensor", "dWO")
                W("tensor", "aO", cnt("aO"))
                W("vector", "gG", G_C1SAVE)
                wob = [pslh(0, 0), pslh(0, 1), pslh(1, 0), pslh(1, 1)]
                for grp in range(2):
                    if grp == 1:
                        W("tensor", "vV")
                    for mt in range(4):
                        dt = grp * 4 + mt
                        for k in range(4):
                            E("tensor", lambda e, b=wob[mt], dt=dt, k=k:
                              e.matmul(b, WO[:, k, dt * 128:(dt + 1) * 128],
                                       OB[:, k, :], start=(k == 0),
                                       stop=(k == 3)), "tM" if k == 3 else None)
                    if grp == 1:
                        wo_mm_mark[0] = cnt("tM")
                    W("vector", "tM")
                    for mt in range(4):
                        dt = grp * 4 + mt
                        E("vector", lambda e, dt=dt, b=wob[mt]: e.tensor_add(
                            Y[:, dt, 1:C + 1], Y[:, dt, 1:C + 1], b), "vV")
                VV_WO = cnt("vV")

                W("gpsimd", "vV", VV_WO)
                G_LEAD2 = E("gpsimd", lambda e, li=li: e.tensor_copy(
                    Y[:, 4:8, 0:1], C2[li][:]), "gG")
                W("gpsimd", "gG", G_LEAD2)
                G_C2SAVE = E("gpsimd", lambda e, li=li: e.tensor_copy(
                    C2[li][:], Y[:, 4:8, C:C + 1]), "gG")

            if KLVL >= 6:
                W("vector", "gG", G_LEAD2)
                for ft in range(8):
                    sl = slice(1, C + 1) if ft < 4 else slice(0, C + 1)
                    E("vector", lambda e, ft=ft, sl=sl: e.tensor_copy(
                        STG[:, ft, sl], Y[:, ft, sl]), "vV" if ft == 7 else None)
                VV_STGD = cnt("vV")

                def shv(k):
                    if k < 4:
                        return STG[:, k, 1:C + 1]
                    return STG[:, k, 0:C]

                W("scalar", "vV", VV_STGD)
                for ft in range(8):
                    E("scalar", lambda e, ft=ft, s=shv(ft): e.activation(
                        SQ16v[:, ft, :], s, AF.Square),
                      "sS" if ft == 7 else None)
                W("tensor", "vV", VV_STGD)
                W("tensor", "sS")
                for k in range(8):
                    E("tensor", lambda e, k=k, s=shv(k): e.matmul(
                        PSL[0][0:1, 0:512], ONESC[:], s,
                        start=(k == 0), stop=(k == 7)))
                for k in range(8):
                    E("tensor", lambda e, k=k: e.matmul(
                        PSL[0][0:1, 512:1024], ONESC[:], SQ16v[:, k, :],
                        start=(k == 0), stop=(k == 7)),
                      "tM" if k == 7 else None)
                ln_rows(PSL[0])
                W("tensor", "vV")
                xn_marks = []
                for dt in range(8 if KD >= 2 else 0):
                    rB = PB[0][:] if dt % 2 == 0 else PB[2][:]
                    sB = PB[1][:] if dt % 2 == 0 else PB[3][:]
                    if dt >= 2:
                        W("tensor", "vV", xn_marks[dt - 2])
                    E("tensor", lambda e, li=li, dt=dt, rB=rB: e.matmul(
                        rB, LGLB[0:1, li, dt * 128:(dt + 1) * 128],
                        RSTD16[:], start=True, stop=True))
                    E("tensor", lambda e, li=li, dt=dt, sB=sB: e.matmul(
                        sB, LGLB[:, li, dt * 128:(dt + 1) * 128],
                        SROW2[:], start=True, stop=True), "tM")
                    W("vector", "tM")
                    ysh = (Y[:, dt, 1:C + 1] if dt < 4 else Y[:, dt, 0:C])
                    E("vector", lambda e, ysh=ysh, rB=rB: e.tensor_mul(
                        TMPF[:], ysh, rB))
                    xn_marks.append(E("vector", lambda e, dt=dt, sB=sB:
                                      e.tensor_add(XNv[:, dt, :], TMPF[:],
                                                   sB), "vV"))
                VV_XN = cnt("vV")

                mb_T, mb_S, mb_V = cnt("mT"), cnt("mS"), cnt("mV")
                for m in range(NM if KD >= 3 else 0):
                    buf = m % 2
                    if m >= 2:
                        W("sync", "mT", mb_T + m - 1)
                    dv = E("sync", lambda e, li=li, m=m, buf=buf: e.dma_start(
                        W1R[:, buf],
                        bass.AP(scr_w1, (li * NM + m) * 128 * 2048,
                                [[2048, 128], [1, 2048]])),
                      "dW1a" if buf == 0 else "dW1b", 16)
                    W("tensor", "dW1a" if buf == 0 else "dW1b", dv)
                    if m == 0:
                        W("tensor", "vV", VV_XN)
                    if m >= 2:
                        W("tensor", "mV", mb_V + m - 1)
                        W("tensor", "mS", mb_S + m - 1)
                    pa, pg = PB[buf][:], PB[2 + buf][:]
                    for k in range(8):
                        E("tensor", lambda e, pa=pa, buf=buf, k=k: e.matmul(
                            pa, W1R[:, buf, 0, k, :], XNv[:, k, :],
                            start=(k == 0), stop=(k == 7)))
                    for k in range(8):
                        E("tensor", lambda e, pg=pg, buf=buf, k=k: e.matmul(
                            pg, W1R[:, buf, 1, k, :], XNv[:, k, :],
                            start=(k == 0), stop=(k == 7)),
                          "mT" if k == 7 else None)
                    W("scalar", "mT")
                    E("scalar", lambda e, li=li, m=m, pg=pg, buf=buf:
                      e.activation(GEGv[:, buf, :], pg, AF.Gelu,
                                   bias=B1G[:, li, m:m + 1]), "mS")
                    W("vector", "mS")
                    E("vector", lambda e, li=li, m=m, pa=pa, buf=buf:
                      e.scalar_tensor_tensor(
                          HBv[:, m, :], pa, B1A[:, li, m:m + 1],
                          GEGv[:, buf, :], ALU.add, ALU.mult), "mV")
                MV_H = cnt("mV")

                w2_marks = {0: None, 1: None}
                for og in range(2 if KD >= 4 else 0):
                    for kg in range(11):
                        kn = 2
                        buf = (og * 11 + kg) % 2
                        if w2_marks[buf] is not None:
                            W("sync", "mT", w2_marks[buf])
                        dv = E("sync", lambda e, li=li, og=og, kg=kg,
                               buf=buf: e.dma_start(
                                   W2R[:, buf],
                                   bass.AP(scr_w2,
                                           ((li * 2 + og) * 11 + kg)
                                           * 128 * 1024,
                                           [[1024, 128], [1, 1024]])),
                               "dW2a" if buf == 0 else "dW2b", 16)
                        W("tensor", "dW2a" if buf == 0 else "dW2b", dv)
                        if kg == 0:
                            if og == 0:
                                W("tensor", "mV", MV_H)
                            else:
                                W("tensor", "vV")
                        for kk in range(kn):
                            kq = kg * 2 + kk
                            for mt in range(4):
                                E("tensor",
                                  lambda e, b=wob[mt], buf=buf, kk=kk,
                                  mt=mt, kq=kq: e.matmul(
                                      b,
                                      W2R[:, buf, kk, mt * 128:(mt + 1) * 128],
                                      HBv[:, kq, :], start=(kq == 0),
                                      stop=(kq == NM - 1)),
                                  "mT" if (kk == kn - 1 and mt == 3) else None)
                        w2_marks[buf] = cnt("mT")
                    W("vector", "mT")
                    W("vector", "gG", G_C2SAVE)
                    for mt in range(4):
                        dt = og * 4 + mt
                        E("vector", lambda e, li=li, dt=dt, b=wob[mt]:
                          e.scalar_tensor_tensor(
                              Y[:, dt, 1:C + 1], b,
                              B2[:, li, dt:dt + 1], Y[:, dt, 1:C + 1],
                              ALU.add, ALU.add), "vV")

        if KLVL >= 0:
            W("vector", "vV")
            for ft in range(8):
                E("vector", lambda e, ft=ft: e.tensor_add(
                    X1[:, ft, :], X1[:, ft, :], Y[:, ft, 1:C + 1]),
                  "vV" if ft == 7 else None)
            VV_W = cnt("vV")
            out_vv_marks = {}
            for tt in range(NT):
                obuf = tt % 2
                if tt == 0:
                    W("tensor", "vV", VV_W)
                    WLAST("tensor", "scalar")
                else:
                    W("tensor", "vV", out_vv_marks[tt - 1])
                for ft in range(8):
                    E("tensor", lambda e, tt=tt, ft=ft: e.transpose(
                        PSL[0][:, ft * 128:(ft + 1) * 128] if ft < 4 else
                        PSL[1][:, (ft - 4) * 128:(ft - 4 + 1) * 128],
                        X1[:, ft, tt * 128:(tt + 1) * 128], IDF[:]),
                      "tM" if ft == 7 else None)
                W("vector", "tM")
                if out_dma_marks[tt % 2] is not None:
                    W("vector", "dOUT0" if tt % 2 == 0 else "dOUT1",
                      out_dma_marks[tt % 2])
                E("vector", lambda e, ob=obuf: e.tensor_copy(
                    OUTB[:, ob, 0:512], PSL[0].ap()[:, 0:512]))
                E("vector", lambda e, ob=obuf: e.tensor_copy(
                    OUTB[:, ob, 512:1024], PSL[1].ap()[:, 0:512]))
                E("vector", lambda e, ob=obuf, tt=tt: e.reduce_sum(
                    SUMC[:, tt:tt + 1], OUTB[:, ob, :], axis=AXX), "vV")
                W("scalar", "vV")
                E("scalar", lambda e, ob=obuf, tt=tt: e.activation(
                    SQDUMP, OUTB[:, ob, :], AF.Square,
                    accum_out=SQC[:, tt:tt + 1]), "sS")
                W("vector", "sS")
                E("vector", lambda e, tt=tt: e.tensor_scalar_mul(
                    MUC[:, tt:tt + 1], SUMC[:, tt:tt + 1], 1.0 / 1024.0))
                E("vector", lambda e, tt=tt: e.tensor_scalar_mul(
                    E2C[:, tt:tt + 1], SQC[:, tt:tt + 1], 1.0 / 1024.0))
                E("vector", lambda e, tt=tt: e.tensor_mul(
                    VARC[:, tt:tt + 1], MUC[:, tt:tt + 1], MUC[:, tt:tt + 1]))
                E("vector", lambda e, tt=tt: e.tensor_sub(
                    VARC[:, tt:tt + 1], E2C[:, tt:tt + 1], VARC[:, tt:tt + 1]))
                E("vector", lambda e, tt=tt: e.tensor_scalar_add(
                    VARC[:, tt:tt + 1], VARC[:, tt:tt + 1], 1e-5))
                E("vector", lambda e, tt=tt: e.reciprocal(
                    RCPC[:, tt:tt + 1], VARC[:, tt:tt + 1]), "vV")
                W("scalar", "vV")
                E("scalar", lambda e, tt=tt: e.activation(
                    RSTDC[:, tt:tt + 1], RCPC[:, tt:tt + 1], AF.Sqrt), "sS")
                W("vector", "sS")
                E("vector", lambda e, tt=tt: e.scalar_tensor_tensor(
                    NMRC[:, tt:tt + 1], MUC[:, tt:tt + 1], -1.0,
                    RSTDC[:, tt:tt + 1], ALU.mult, ALU.mult), "vV")
                W("scalar", "vV")
                E("scalar", lambda e, ob=obuf, tt=tt: e.activation(
                    OUTB[:, ob, :], OUTB[:, ob, :], AF.Identity,
                    scale=RSTDC[:, tt:tt + 1], bias=NMRC[:, tt:tt + 1]), "sS")
                out_vv_marks[tt] = cnt("vV")
                W("sync", "sS")
                out_dma_marks[tt % 2] = E(
                    "sync", lambda e, ob=obuf, tt=tt, t0=t0: e.dma_start(
                        out_h.ap()[t0 + tt * 128:t0 + (tt + 1) * 128, :],
                        OUTB[:, ob, :]), "dOUT0" if tt % 2 == 0 else "dOUT1", 16)
    W("sync", "dOUT0")
    W("sync", "dOUT1")

    with nc.Block() as block:
        @block.sync
        def _(e):
            for f in ops["sync"]:
                f(e)

        @block.gpsimd
        def _(e):
            for f in ops["gpsimd"]:
                f(e)

        @block.tensor
        def _(e):
            for f in ops["tensor"]:
                f(e)

        @block.vector
        def _(e):
            for f in ops["vector"]:
                f(e)

        @block.scalar
        def _(e):
            for f in ops["scalar"]:
                f(e)

    return nc


def _prep(inputs):
    g = lambda k: np.asarray(inputs[k])
    f32 = lambda x: np.ascontiguousarray(x, dtype=np.float32)
    w1 = g("L0_ffn_w1")
    w2 = g("L0_ffn_w2")
    b1 = g("L0_ffn_b1")
    HID = 2730
    pad1 = HIDP - HID
    w1a = np.concatenate([w1[:, :, :HID],
                          np.zeros((2, D, pad1), np.float32)], axis=2)
    w1g = np.concatenate([w1[:, :, HID:],
                          np.zeros((2, D, pad1), np.float32)], axis=2)
    w2p = np.concatenate([w2, np.zeros((2, pad1, D), np.float32)], axis=1)
    b1a = np.concatenate([b1[:, :HID], np.zeros((2, pad1), np.float32)], 1)
    b1g = np.concatenate([b1[:, HID:], np.zeros((2, pad1), np.float32)], 1)
    qs = np.ascontiguousarray(np.broadcast_to(
        np.tile(g("L0_ffn_q_scale"), (1, H))[:, None, :],
        (2, 128, 512)).astype(np.float32))
    ks = np.ascontiguousarray(np.broadcast_to(
        np.tile(g("L0_ffn_k_scale"), (1, H))[:, None, :],
        (2, 128, 512)).astype(np.float32))
    lglb = np.stack([g("L0_ffn_ln_g"), g("L0_ffn_ln_b")], axis=0)
    return dict(
        bwT=f32(g("L0_attn_B_w").T),
        cwT=f32(g("L0_attn_C_w").T),
        llam=f32(g("L0_attn_log_lambda").reshape(2, 128).T),
        dvec=f32(g("L0_attn_D").reshape(8, 128).T),
        wqkv=f32(g("L0_ffn_wqkv")),
        wo=f32(g("L0_ffn_wo")),
        w1a=f32(w1a), w1g=f32(w1g), w2p=f32(w2p),
        b1a=f32(b1a.reshape(2, NM, 128).transpose(0, 2, 1)),
        b1g=f32(b1g.reshape(2, NM, 128).transpose(0, 2, 1)),
        b2=f32(g("L0_ffn_b2").reshape(2, 8, 128).transpose(0, 2, 1)),
        lglb=f32(lglb),
        qsc=qs, ksc=ks,
        idb=np.eye(128, dtype=BF),
        idf=np.eye(128, dtype=np.float32),
        onescol=np.ones((128, 1), dtype=BF),
        onesrow=np.ones((1, 128), dtype=BF),
        ones2r=np.ones((1, 512), dtype=BF),
        masks=host_masks(),
        mcut=host_mcut(),
        rtab=np.ascontiguousarray(np.stack(host_rotary_tables())),
    )


def kernel(**inputs):
    shared = _prep(inputs)
    u = np.asarray(inputs["L0_attn_u"])
    nc = build(T // C)
    in_maps = []
    for c in range(8):
        m = dict(shared)
        m["u"] = np.ascontiguousarray(u[c], dtype=np.float32)
        in_maps.append(m)
    res = run_bass_kernel_spmd(nc, in_maps, core_ids=list(range(8)))
    return np.stack([np.asarray(r["out"], dtype=np.float32)
                     for r in res.results])
